# version 13
# baseline (speedup 1.0000x reference)
import numpy as np
import ml_dtypes
import concourse.bass as bass
import concourse.mybir as mybir
from concourse.bass_utils import run_bass_kernel_spmd

F32 = mybir.dt.float32
BF16 = mybir.dt.bfloat16
AF = mybir.ActivationFunctionType
ALU = mybir.AluOpType

T = 1024
NT = 8
D = 2048
ND = 16
H = 16
KV = 4
DK = 128
HID = 5632
NH = 44
SEQ = 2048
NKT = 16
LN_EPS = 1e-5
FFN_EPS = 1e-8
KILL = -1.0e5

_TAPS = []

BF = ml_dtypes.bfloat16


class Op:
    __slots__ = ("stream", "fn", "deps", "needed", "idx", "kind", "tag", "ccname")

    def __init__(self, stream, fn, idx, kind, tag):
        self.stream = stream
        self.fn = fn
        self.deps = []
        self.needed = False
        self.idx = idx
        self.kind = kind
        self.tag = tag


class P:
    STREAM_ENGINE = {"te": "tensor", "ve": "vector", "se": "scalar", "ge": "gpsimd",
                     "qd": "sync", "gd": "gpsimd", "sd": "scalar", "cc": "gpsimd"}

    def __init__(self, nc):
        self.nc = nc
        self.ops = []
        self.res = {}
        self.cc_n = 0

    def _entry(self, r):
        e = self.res.get(r)
        if e is None:
            e = self.res[r] = {"w": None, "r": [], "pre": []}
        return e

    def _touch(self, op, reads, writes):
        dep = lambda d: (d is not op and d not in op.deps) and op.deps.append(d)
        for r in reads:
            e = self._entry(r)
            if e["w"] is not None:
                dep(e["w"])
            for d in e["pre"]:
                dep(d)
            e["pre"] = []
            e["r"].append(op)
        for r in writes:
            e = self._entry(r)
            if e["w"] is not None:
                dep(e["w"])
            for d in e["r"]:
                dep(d)
            for d in e["pre"]:
                dep(d)
            self.res[r] = {"w": op, "r": [], "pre": []}

    def carry(self, new_res, old_ops):
        e = self._entry(new_res)
        e["pre"] = list(old_ops)

    def deps_of(self, res_names):
        out = []
        for r in res_names:
            e = self.res.get(r)
            if e is None:
                continue
            if e["w"] is not None:
                out.append(e["w"])
            out.extend(e["r"])
        return out

    def do(self, stream, fn, r=(), w=(), tag=""):
        op = Op(stream, fn, len(self.ops), "c", tag)
        self._touch(op, r, w)
        self.ops.append(op)
        return op

    def dma(self, stream, out, in_, r=(), w=(), tag=""):
        op = Op(stream, lambda e: e.dma_start(out=out, in_=in_), len(self.ops), "d", tag)
        self._touch(op, r, w)
        self.ops.append(op)
        return op

    def cc(self, kind, aluop, groups, ins, outs, r=(), w=(), tag=""):
        def fn(e):
            return e.collective_compute(kind, aluop, replica_groups=groups,
                                        ins=ins, outs=outs)
        op = Op("cc", fn, len(self.ops), "cc", tag)
        self._touch(op, r, w)
        self.ops.append(op)
        return op

    DMA_ROT = 8

    def emit(self, final_out_dmas=()):
        nc = self.nc
        for op in self.ops:
            if op.kind == "d":
                op.needed = True
            for d in op.deps:
                if d.kind == "c" and d.stream == op.stream == "te":
                    continue
                d.needed = True
        for op in final_out_dmas:
            op.needed = True
        counts = {}
        comp = {}
        throttle = {}
        for op in self.ops:
            if op.kind == "cc":
                self.cc_n += 1
                op.ccname = "cc%d" % self.cc_n
                comp[op] = (op.ccname, 1)
                continue
            if op.kind == "d":
                k = counts.get(op.stream, 0)
                counts[op.stream] = k + 1
                s = "%s%d" % (op.stream, k % self.DMA_ROT)
                comp[op] = (s, 16 * (k // self.DMA_ROT + 1))
                if k >= self.DMA_ROT:
                    throttle[op] = (s, 16 * (k // self.DMA_ROT))
                continue
            if op.needed:
                c = counts.get(op.stream, 0) + 1
                counts[op.stream] = c
                comp[op] = (op.stream, c)
            else:
                comp[op] = None
        sem_names = set()
        for v in comp.values():
            if v is not None:
                sem_names.add(v[0])
        import contextlib
        self.stack = contextlib.ExitStack()
        sems = {s: self.stack.enter_context(nc.semaphore("sem_" + s))
                for s in sorted(sem_names)}
        engines = {"tensor": nc.tensor, "vector": nc.vector, "scalar": nc.scalar,
                   "gpsimd": nc.gpsimd, "sync": nc.sync}
        known = {e: {} for e in engines}
        n_waits = 0
        with nc.Block():
            for op in self.ops:
                eng_name = self.STREAM_ENGINE[op.stream]
                eng = engines[eng_name]
                need = {}
                th = throttle.get(op)
                if th is not None:
                    need[th[0]] = max(need.get(th[0], 0), th[1])
                for dop in op.deps:
                    c = comp[dop]
                    if c is None:
                        continue
                    if dop.kind == "c" and dop.stream == op.stream == "te":
                        continue
                    s, v = c
                    need[s] = max(need.get(s, 0), v)
                for s, v in sorted(need.items()):
                    if known[eng_name].get(s, 0) >= v:
                        continue
                    eng.wait_ge(sems[s], v)
                    known[eng_name][s] = v
                    n_waits += 1
                inst = op.fn(eng)
                c = comp[op]
                if c is not None:
                    s, v = c
                    inst.then_inc(sems[s], 16 if op.kind == "d" else 1)
            fin = {}
            for op in final_out_dmas:
                s, v = comp[op]
                fin[s] = max(fin.get(s, 0), v)
            for s, v in sorted(fin.items()):
                nc.sync.wait_ge(sems[s], v)
        self.n_waits = n_waits


class Arena:

    def __init__(self, nc, p, base, top):
        self.nc, self.p = nc, p
        self.base, self.top = base, top
        self.ptr = base
        self.live = []
        self.uid = 0

    def pos(self):
        return self.ptr

    def seek(self, off):
        self.ptr = off

    def reset(self):
        self.ptr = self.base

    def alloc(self, name, shape, dtype):
        self.uid += 1
        name = "ov_%s_%d" % (name, self.uid)
        nb = int(np.prod(shape[1:])) * mybir.dt.size(dtype)
        nb = (nb + 31) // 32 * 32
        s, e = self.ptr, self.ptr + nb
        assert e <= self.top, "arena overflow at %s (%d + %d > %d)" % (name, s, nb, self.top)
        t = self.nc.alloc_sbuf_tensor_at(name, shape, dtype, offset=s)
        self.ptr = e
        killed = [r for r in self.live if r[0] < e and s < r[1]]
        if killed:
            self.p.carry(name, self.p.deps_of([r[2] for r in killed]))
            keep = [r for r in self.live if r not in killed]
            for (ks, ke, kn) in killed:
                if ks < s:
                    keep.append((ks, s, kn))
                if ke > e:
                    keep.append((e, ke, kn))
            self.live = keep
        self.live.append((s, e, name))
        return t, name

    def alloc_n(self, name, n, shape, dtype):
        return [self.alloc("%s%d" % (name, i), shape, dtype) for i in range(n)]


def host_constants(core_id):
    half = core_id % 2
    c = {}
    c["ident_bf16"] = np.eye(128, dtype=BF)
    c["ones_row_bf"] = np.ones((1, 128), dtype=BF)
    c["half_row_bf"] = np.full((1, 128), 0.5, dtype=BF)
    c["ones_col_f32"] = np.ones((128, 1), dtype=np.float32)
    c["ones_row_f32"] = np.ones((1, 128), dtype=np.float32)
    c["ones_col_bf"] = np.ones((128, 1), dtype=BF)
    s_i = np.arange(128)[:, None]
    t_i = np.arange(128)[None, :]
    c["tri_gla"] = (s_i <= t_i).astype(BF)
    bm = np.zeros((128, 4 * 512), dtype=BF)
    qq = np.arange(512)[None, :]
    for i in range(4):
        bm[:, 512 * i:512 * (i + 1)] = ((128 * i + s_i) <= qq).astype(BF)
    c["band_masks"] = bm
    bt = np.zeros((128, 32), dtype=np.float32)
    for j in range(2):
        for ck in range(16):
            if ck < 8:
                dead = ck > 4 * j + 3
            else:
                dead = (half == 0)
            if dead:
                bt[:, 16 * j + ck] = KILL
    c["bias_tbl"] = bt
    inv = 1.0 / 10000.0 ** (np.arange(0, 128, 2, dtype=np.float64) / 128)
    pos = np.arange(1024 * half, 1024 * half + 1024, dtype=np.float64)
    fr = pos[None, :] * inv[:, None]
    c["rope_cos"] = np.vstack([np.cos(fr), np.cos(fr)]).astype(BF)
    c["rope_sin"] = np.vstack([-np.sin(fr), np.sin(fr)]).astype(BF)
    c["gate_col"] = np.full((128, 1), float(half), dtype=np.float32)
    c["zeros128"] = np.zeros((128, 128), dtype=np.float32)
    bi = np.zeros((4, 128), dtype=BF)
    for r_ in range(4):
        bi[r_, 32 * r_:32 * (r_ + 1)] = 1
    c["blkind"] = bi
    sel = np.zeros((64, 2048), dtype=BF)
    for cc in range(16):
        for pp in range(128):
            sel[4 * cc + pp // 32, 128 * cc + pp] = 1
    c["selscale"] = sel
    return c


CONST_SPECS = [
    ("ident_bf16", [128, 128], BF16),
    ("ones_row_bf", [1, 128], BF16), ("half_row_bf", [1, 128], BF16),
    ("ones_col_f32", [128, 1], F32), ("ones_row_f32", [1, 128], F32), ("ones_col_bf", [128, 1], BF16),
    ("tri_gla", [128, 128], BF16), ("band_masks", [128, 2048], BF16),
    ("bias_tbl", [128, 32], F32),
    ("rope_cos", [128, 1024], BF16), ("rope_sin", [128, 1024], BF16),
    ("gate_col", [128, 1], F32), ("zeros128", [128, 128], F32),
    ("blkind", [4, 128], BF16), ("selscale", [64, 2048], BF16),
]


class Build:
    def __init__(self, taps=()):
        self.taps = list(taps)
        nc = self.nc = bass.Bass("TRN2", target_bir_lowering=False)
        self.p = P(nc)
        self.uid = 0
        self.tap_outs = {}
        self.final_dmas = []
        self.declare_io()
        self.alloc()
        self.body()
        self.p.emit(final_out_dmas=self.final_dmas)
        nc.finalize()

    def pjp(self, i):
        if i % 2 == 0:
            return [(self.pw[0][:, 0:512], "pw0"), (self.pw[0][:, 512:1024], "pw0")]
        return [(self.pw[1][:, :], "pw1"), (self.pdn[:, :], "pdn")]

    def declare_io(self):
        nc = self.nc
        dr = lambda n, s, dt=F32: nc.dram_tensor(n, s, dt, kind="ExternalInput").ap()
        self.i_x = dr("x_fm", [D, T])
        self.i_wq = dr("wq", [D, D])
        self.i_wk = dr("wk", [D, 512])
        self.i_wv = dr("wv", [D, 512])
        self.i_wo = dr("wo", [D, D])
        self.i_codes = {k: dr("codes_" + k, [D, D]) for k in "qkvo"}
        self.i_scales = {k: dr("scales_" + k, [64, D]) for k in "qkvo"}
        self.i_bias = {k: dr("bias_" + k, [128, 16]) for k in "qkvo"}
        self.i_w1t = dr("w1t", [HID, D])
        self.i_w2t = dr("w2t", [HID, D])
        self.i_w3t = dr("w3t", [HID, D])
        self.i_wsl = [dr("wsl%d" % i, [128, 11264]) for i in range(3)]
        self.consts = {n: dr(n, s, dt) for (n, s, dt) in CONST_SPECS}
        self.o_y = nc.dram_tensor("y", [D, T], F32, kind="ExternalOutput").ap()
        di = lambda n, s, dt: nc.dram_tensor(n, s, dt, kind="Internal")
        self.d_gamma_in = di("gamma_in", [1, 4], F32)
        self.d_gamma_out = di("gamma_out", [1, 4], F32)
        self.d_akv_in = di("akv_in", [1024, 1024], BF16)
        self.d_akv_out = di("akv_out", [2048, 1024], BF16)
        self.d_as_in = di("as_in", [512, 128], F32)
        self.d_as_out = di("as_out", [1024, 128], F32)
        self.d_bk_in = di("bk_in", [2048, 1024], BF16)
        self.d_bk_out = di("bk_out", [4096, 1024], BF16)
        self.d_bv_in = di("bv_in", [1024, 2048], BF16)
        self.d_bv_out = di("bv_out", [2048, 2048], BF16)
        self.d_qstage = di("qstage", [D, T], BF16)
        self.d_qw1 = di("qw1", [HID, D], BF16)
        self.d_qw2 = di("qw2", [HID, D], BF16)
        self.d_qw3 = di("qw3", [HID, D], BF16)

    def alloc(self):
        nc = self.nc
        sb = nc.alloc_sbuf_tensor
        self.stream = [sb("stream%d" % i, [128, T], F32) for i in range(ND)]
        self.rowf = sb("rowf", [1, 3 * T], F32)
        self.rowb = sb("rowb", [1, 2 * T], BF16)
        self.S_f = [sb("Sf%d" % i, [128, 128], F32) for i in range(KV)]
        self.S_tmp = sb("Stmp", [128, 128], F32)
        self.S_b = [[sb("Sb%d_%d" % (k, c), [128, 128], BF16) for c in range(NT)] for k in range(KV)]
        self.S_in = [sb("Sin%d" % k, [128, 128], F32) for k in range(KV)]
        self.colA = sb("colA", [128, 32], F32)
        self.colB = sb("colB", [128, 32], F32)
        self.colC = sb("colC", [128, 16], F32)
        self.gsc = sb("gsc", [1, 12], F32)
        self.bct = {k: sb("bct_" + k, [128, 16], F32) for k in "qkvo"}
        self.cst = {}
        for (n, s, dt) in CONST_SPECS:
            self.cst[n] = sb("c_" + n, s, dt)
        ps = nc.alloc_psum_tensor
        self.pb = [ps("pb%d" % i, [128, 512], F32) for i in range(3)]
        self.pbb = ps("pbb", [128, 1024], BF16)
        self.pw = [ps("pw0", [128, 1024], F32), ps("pw1", [128, 512], F32)]
        self.pdn = ps("pdn", [128, 512], F32)
        base = nc.sbuf_base
        top = nc.sbuf_top
        self.ar = Arena(nc, self.p, base, top)
        self.arena_kb = (top - base) // 1024

    def body(self):
        p = self.p
        for (n, s, dt) in CONST_SPECS:
            p.dma("qd", self.cst[n][:, :], self.consts[n], w=["c_" + n], tag="ldc")
        for i in range(ND):
            p.dma("qd", self.stream[i][:, :], self.i_x[128 * i:128 * (i + 1), :],
                  w=["st%d" % i], tag="ldx")
        self.ar.reset()
        self.gamma_prep()
        self.ar.reset()
        xln = self.layernorm(1)
        self.tap_big("ln1", xln)
        self.attn_a(xln)
        self.tap_big("x1", [(self.stream[i], "st%d" % i) for i in range(ND)], f32=True)
        self.ar.reset()
        xln = self.layernorm(2)
        self.ffn(xln, first=True)
        self.tap_big("x2", [(self.stream[i], "st%d" % i) for i in range(ND)], f32=True)
        self.ar.reset()
        xln = self.layernorm(3)
        self.attn_b(xln)
        self.tap_big("x3", [(self.stream[i], "st%d" % i) for i in range(ND)], f32=True)
        self.ar.reset()
        xln = self.layernorm(4)
        self.ffn(xln, first=False)
        for i in range(ND):
            dma = p.dma("qd", self.o_y[128 * i:128 * (i + 1), :], self.stream[i][:, :],
                        r=["st%d" % i], w=["out%d" % i], tag="sty")
            self.final_dmas.append(dma)

    def attn_b(self, xln):
        p, cst, ar = self.p, self.cst, self.ar
        xr = [x[1] for x in xln]
        ar.seek(self.xln_end)
        weff2 = [ar.alloc_n("weff%d" % i2, 16, [128, 512], BF16) for i2 in range(2)]
        cstg = ar.alloc_n("cstg", 2, [128, 512], F32)
        sstg = ar.alloc_n("sstg", 1, [64, D], F32)
        ssbf = ar.alloc_n("ssbf", 2, [64, D], BF16)
        qev = ar.alloc_n("bqev", 2, [128, T], BF16)
        bcols = {}
        for k in "qkvo":
            bt, btn = self.bct[k], "bct_" + k
            bcols[k] = (bt, btn)
            p.dma("qd", bt[:, :], self.i_bias[k], w=[btn], tag="bld")

        def dequant_group(key, cols, tag, pools=None, g=0):
            if pools is None:
                pools = (weff2[g % 2], cstg, sstg, ssbf)
            ret = pools[0]
            weff_, cstg_, sstg_, ssbf_ = pools
            ss, ssn = sstg_[0]
            sb_, sbn = ssbf_[0 if len(ssbf_) == 1 else ("qkvo".index(key) % 2)]
            if cols[0] == 0:
                p.dma("gd", ss[:, :], self.i_scales[key][:, :], w=[ssn], tag="sld" + tag)
                p.do("ve", lambda e, ss=ss, sb_=sb_: e.tensor_copy(out=sb_[:, :], in_=ss[:, :]),
                     r=[ssn], w=[sbn], tag="scst" + tag)
            for c in range(ND):
                cs, csn = cstg_[c % 2]
                wf, wfn = weff_[c]
                p.dma("qd" if c % 2 == 0 else "gd", cs[:, :],
                      self.i_codes[key][128 * c:128 * (c + 1), cols[0]:cols[1]],
                      w=[csn], tag="cld" + tag)
                p.do("te", lambda e, sb_=sb_, c=c, cols=cols: e.matmul(self.pb[2][:, :],
                        cst["selscale"][:, 128 * c:128 * (c + 1)],
                        sb_[:, cols[0]:cols[1]], start=True, stop=True),
                     r=[sbn, "c_selscale"], w=["pb2"], tag="sbc" + tag)
                p.do("ve", lambda e, cs=cs, wf=wf: e.tensor_mul(out=wf[:, :], in0=cs[:, :],
                                                                in1=self.pb[2][:, :]),
                     r=[csn, "pb2"], w=[wfn], tag="deq" + tag)
            return ret

        for g in range(4):
            weff = dequant_group("k", (512 * g, 512 * (g + 1)), "k", g=g)
            for sl4 in range(4):
                sl = 4 * g + sl4
                pair = self.pjp(sl4)
                for t in range(2):
                    pap, prs = pair[t]
                    for c in range(ND):
                        wf, wfn = weff[c]
                        p.do("te", lambda e, sl4=sl4, t=t, c=c, wf=wf, pap=pap: e.matmul(
                                pap, wf[:, 128 * sl4:128 * (sl4 + 1)],
                                xln[c][0][:, 512 * t:512 * (t + 1)],
                                start=(c == 0), stop=(c == ND - 1)),
                             r=[wfn, xr[c]], w=[prs], tag="bkproj")
                qe, qen = qev[sl4 % 2]
                for t in range(2):
                    pap, prs = pair[t]
                    p.do("se", lambda e, qe=qe, sl=sl, pap=pap, t=t: e.activation(
                            out=qe[:, 512 * t:512 * (t + 1)], in_=pap, func=AF.Identity,
                            bias=bcols["k"][0][:, sl:sl + 1]),
                         r=[prs, bcols["k"][1]], w=[qen], tag="bkevac")
                p.dma("gd", self.d_bk_in[128 * sl:128 * (sl + 1), :], qe[:, :],
                      r=[qen], w=["bk_in"], tag="bkst")

        vreg = self.d_bv_in.ap()
        for g in range(4):
            weff = dequant_group("v", (512 * g, 512 * (g + 1)), "v", g=g)
            for tc in range(NT):
                for c in range(ND):
                    wf, wfn = weff[c]
                    p.do("te", lambda e, tc=tc, c=c, wf=wf: e.matmul(
                            self.pb[tc % 2][:, :], xln[c][0][:, 128 * tc:128 * (tc + 1)], wf[:, :],
                            start=(c == 0), stop=(c == ND - 1)),
                         r=[wfn, xr[c]], w=["pb%d" % (tc % 2)], tag="bvproj")
                qe, qen = qev[tc % 2]
                p.do("se", lambda e, qe=qe, tc=tc: e.activation(
                        out=qe[:, 0:512], in_=self.pb[tc % 2][:, :], func=AF.Copy),
                     r=["pb%d" % (tc % 2)], w=[qen], tag="bvevac")
                dst = vreg[:, 512 * g:512 * (g + 1)][128 * tc:128 * (tc + 1), :]
                p.dma("gd", dst, qe[:, 0:512], r=[qen], w=["bv_in"], tag="bvst")

        ccb1 = p.cc("AllGather", ALU.bypass, [[0, 1], [2, 3], [4, 5], [6, 7]],
                    [self.d_bk_in.ap()], [self.d_bk_out.ap()],
                    r=["bk_in"], w=["bk_out"], tag="ccbk")
        ccb2 = p.cc("AllGather", ALU.bypass, [[0, 1], [2, 3], [4, 5], [6, 7]],
                    [self.d_bv_in.ap()], [self.d_bv_out.ap()],
                    r=["bv_in"], w=["bv_out"], tag="ccbv")

        for g in range(4):
            weff = dequant_group("q", (512 * g, 512 * (g + 1)), "q", g=g)
            for sl4 in range(4):
                sl = 4 * g + sl4
                pair = self.pjp(sl4)
                for t in range(2):
                    pap, prs = pair[t]
                    for c in range(ND):
                        wf, wfn = weff[c]
                        p.do("te", lambda e, sl4=sl4, t=t, c=c, wf=wf, pap=pap: e.matmul(
                                pap, wf[:, 128 * sl4:128 * (sl4 + 1)],
                                xln[c][0][:, 512 * t:512 * (t + 1)],
                                start=(c == 0), stop=(c == ND - 1)),
                             r=[wfn, xr[c]], w=[prs], tag="bqproj")
                qe, qen = qev[sl4 % 2]
                for t in range(2):
                    pap, prs = pair[t]
                    p.do("se", lambda e, qe=qe, sl=sl, pap=pap, t=t: e.activation(
                            out=qe[:, 512 * t:512 * (t + 1)], in_=pap, func=AF.Identity,
                            bias=bcols["q"][0][:, sl:sl + 1]),
                         r=[prs, bcols["q"][1]], w=[qen], tag="bqevac")
                p.dma("gd", self.d_qstage[128 * sl:128 * (sl + 1), :], qe[:, :],
                      r=[qen], w=["qst%d" % sl], tag="bqstg")

        ar.seek(self.xln_end)
        kbuf = ar.alloc_n("bk", 2, [128, SEQ], BF16)
        vbuf = ar.alloc_n("bv", 2, [128, SEQ], BF16)
        qbuf = ar.alloc_n("bq", 2, [128, T], BF16)
        ep = ar.alloc_n("bep", 3, [128, 512], BF16)
        ysb = ar.alloc("bysb", [128, 512], BF16)
        ar.seek(self.ar.base)
        obig = [ar.alloc("bobig%d" % h2, [128, T], BF16) for h2 in range(H)]
        self.bcols_v = bcols["v"]
        vflat = self.d_bv_out.ap()
        R = lambda i: self.rowf[0:1, T * i:T * (i + 1)]
        RB = lambda i: self.rowb[0:1, T * i:T * (i + 1)]
        self.pending = []
        for h in range(H):
            kb, kbn = kbuf[h % 2]
            vb, vbn = vbuf[h % 2]
            qb_, qbn = qbuf[h % 2]
            p.dma("qd", qb_[:, :], self.d_qstage[128 * h:128 * (h + 1), :],
                  r=["qst%d" % h], w=[qbn], tag="bldq")
            for r_ in range(2):
                p.dma("qd", kb[:, T * r_:T * (r_ + 1)],
                      self.d_bk_out[2048 * r_ + 128 * h:2048 * r_ + 128 * (h + 1), :],
                      r=["bk_out"], w=[kbn], tag="bldk")
                vsec = vflat[1024 * r_:1024 * (r_ + 1), 128 * h:128 * (h + 1)]
                src = vsec.rearrange("(tc pp) j -> pp tc j", pp=128)
                dstv = vb[:, 1024 * r_:1024 * (r_ + 1)].rearrange("p (tc j) -> p tc j", tc=8)
                p.dma("qd", dstv, src, r=["bv_out"], w=[vbn], tag="bldv")
            for j in range(2):
                sp = (2 * h + j) % 2
                qsl = qb_[:, 512 * j:512 * (j + 1)]

                def score(ck, kb=kb, kbn=kbn, qsl=qsl, qbn=qbn):
                    pbi = ck % 2
                    p.do("te", lambda e, kb=kb, ck=ck, qsl=qsl, pbi=pbi: e.matmul(
                            self.pb[pbi][:, :], kb[:, 128 * ck:128 * (ck + 1)], qsl,
                            start=True, stop=True),
                         r=[kbn, qbn], w=["pb%d" % pbi], tag="bscore")

                score(0)
                for ck in range(NKT):
                    if ck + 1 < NKT:
                        score(ck + 1)
                    pbi = ck % 2
                    epd, epdn = ep[ck % 3]
                    p.do("se", lambda e, pbi=pbi, epd=epd: e.activation(
                            out=epd[:, :], in_=self.pb[pbi][:, :], func=AF.Exp,
                            scale=float(DK) ** -0.5),
                         r=["pb%d" % pbi], w=[epdn], tag="bexp")
                    p.do("te", lambda e, vb=vb, ck=ck, epd=epd, j=j: e.matmul(
                            self.pw[0][:, 512 * j:512 * (j + 1)], vb[:, 128 * ck:128 * (ck + 1)],
                            epd[:, :], start=(ck == 0), stop=(ck == NKT - 1)),
                         r=[vbn, epdn], w=["pw0"], tag="bymm")
                    p.do("te", lambda e, epd=epd, ck=ck: e.matmul(
                            self.pdn[0:1, :], cst["ones_col_bf"][:, :], epd[:, :],
                            start=(ck == 0), stop=(ck == NKT - 1)),
                         r=[epdn, "c_ones_col_bf"], w=["pdn"], tag="bdnp")
                p.do("ve", lambda e, sp=sp: e.reciprocal(out=R(1 + sp)[:, 0:512], in_=self.pdn[0:1, :]),
                     r=["pdn"], w=["rowS%d" % (1 + sp)], tag="bdnrec")
                p.do("ve", lambda e, sp=sp: e.tensor_copy(out=RB(1)[:, 512 * sp:512 * (sp + 1)],
                                                          in_=R(1 + sp)[:, 0:512]),
                     r=["rowS%d" % (1 + sp)], w=["rowB1"], tag="bdnrecb")
                self.emit_pending()
                self.pending.append(self.fin_quad(h, j, sp, ysb, None, obig[h], half=False))
        while self.pending:
            self.emit_pending()
        self.tap_big("b_y", obig)

        wog = ar.alloc_n("wog", 16, [128, 512], BF16)
        cstg2 = ar.alloc_n("cstg2", 2, [128, 512], F32)
        sstg2 = ar.alloc_n("sstg2", 1, [64, D], F32)
        ssbf2 = ar.alloc_n("ssbf2", 1, [64, D], BF16)
        pools2 = (wog, cstg2, sstg2, ssbf2)
        for g in range(4):
            dequant_group("o", (512 * g, 512 * (g + 1)), "o", pools=pools2)
            for dsl in range(4):
                ds = 4 * g + dsl
                pair = self.pjp(dsl)
                for t in range(2):
                    pap, prs = pair[t]
                    for hh in range(H):
                        wf, wfn = wog[hh]
                        p.do("te", lambda e, hh=hh, t=t, dsl=dsl, wf=wf, pap=pap: e.matmul(
                                pap, wf[:, 128 * dsl:128 * (dsl + 1)],
                                obig[hh][0][:, 512 * t:512 * (t + 1)],
                                start=(hh == 0), stop=(hh == H - 1)),
                             r=[wfn, obig[hh][1]], w=[prs], tag="bwo")
                for t in range(2):
                    pap, prs = pair[t]
                    p.do("ve", lambda e, ds=ds, pap=pap, t=t: e.scalar_tensor_tensor(
                            out=self.stream[ds][:, 512 * t:512 * (t + 1)], in0=pap,
                            scalar=bcols["o"][0][:, ds:ds + 1],
                            in1=self.stream[ds][:, 512 * t:512 * (t + 1)],
                            op0=ALU.add, op1=ALU.add),
                         r=[prs, "st%d" % ds, bcols["o"][1]], w=["st%d" % ds], tag="bresadd")

    def gamma_prep(self):
        p, cst, ar = self.p, self.cst, self.ar
        ar.seek(self.ar.base + 88 * 1024)
        stg = ar.alloc_n("gstg", 2, [128, 1408], F32)
        CW = "colC"
        for w in range(3):
            for pc in range(8):
                sg, sgn = stg[pc % 2]
                p.dma("qd", sg[:, :], self.i_wsl[w][:, 1408 * pc:1408 * (pc + 1)],
                      w=[sgn], tag="gld")
                dst = self.colC[:, 8 + w:9 + w] if pc == 0 else self.colC[:, 11:12]
                p.do("ve", lambda e, sg=sg, dst=dst: e.reduce_sum(
                        out=dst, in_=sg[:, :], axis=mybir.AxisListType.X,
                        apply_absolute_value=True),
                     r=[sgn], w=[CW], tag="gred")
                if pc > 0:
                    p.do("ve", lambda e, w=w: e.tensor_add(
                            out=self.colC[:, 8 + w:9 + w], in0=self.colC[:, 8 + w:9 + w],
                            in1=self.colC[:, 11:12]),
                         r=[CW], w=[CW], tag="gacc")
            p.do("te", lambda e, w=w: e.matmul(self.pb[2][0:1, w:w + 1],
                    self.colC[:, 8 + w:9 + w], cst["ones_col_f32"][:, :],
                    start=True, stop=True),
                 r=[CW, "c_ones_col_f32"], w=["pb2"], tag="gmm")
        p.do("ve", lambda e: e.tensor_copy(out=self.gsc[0:1, 0:3], in_=self.pb[2][0:1, 0:3]),
             r=["pb2"], w=["gsc"], tag="gcp")
        p.do("ve", lambda e: e.memset(self.gsc[0:1, 3:4], 0.0), w=["gsc"], tag="gz")
        p.dma("gd", self.d_gamma_in[0:1, :], self.gsc[0:1, 0:4], r=["gsc"], w=["g_in"], tag="gst")
        p.cc("AllReduce", ALU.add, [[0, 1, 2, 3, 4, 5, 6, 7]],
             [self.d_gamma_in.ap()], [self.d_gamma_out.ap()],
             r=["g_in"], w=["g_out"], tag="cc0")
        p.dma("qd", self.gsc[0:1, 4:8], self.d_gamma_out[0:1, :], r=["g_out"], w=["gsc"], tag="gbk")
        NW = float(HID) * float(D)
        p.do("ve", lambda e: e.tensor_scalar(out=self.gsc[0:1, 8:11], in0=self.gsc[0:1, 4:7],
                scalar1=0.5 / NW, scalar2=0.5 * FFN_EPS, op0=ALU.mult, op1=ALU.add),
             r=["gsc"], w=["gsc"], tag="gth")
        p.do("ve", lambda e: e.tensor_scalar(out=self.gsc[0:1, 4:7], in0=self.gsc[0:1, 8:11],
                scalar1=-1.0, scalar2=None, op0=ALU.mult),
             r=["gsc"], w=["gsc"], tag="gneg")
        for w in range(3):
            p.do("te", lambda e, w=w: e.matmul(self.pb[2][:, 8 + w:9 + w],
                    cst["ones_row_f32"][:, :], self.gsc[0:1, 8 + w:9 + w],
                    start=True, stop=True),
                 r=["gsc", "c_ones_row_f32"], w=["pb2"], tag="gbc")
            p.do("ve", lambda e, w=w: e.tensor_copy(out=self.colC[:, 4 + w:5 + w],
                                                    in_=self.pb[2][:, 8 + w:9 + w]),
                 r=["pb2"], w=[CW], tag="gbcc")
            p.do("te", lambda e, w=w: e.matmul(self.pb[2][:, 12 + w:13 + w],
                    cst["ones_row_f32"][:, :], self.gsc[0:1, 4 + w:5 + w],
                    start=True, stop=True),
                 r=["gsc", "c_ones_row_f32"], w=["pb2"], tag="gbcn")
            p.do("ve", lambda e, w=w: e.tensor_copy(out=self.colC[:, 8 + w:9 + w],
                                                    in_=self.pb[2][:, 12 + w:13 + w]),
                 r=["pb2"], w=[CW], tag="gbccn")

    def ffn(self, xln, first):
        p, cst, ar = self.p, self.cst, self.ar
        xr = [x[1] for x in xln]
        u = "f1" if first else "f2"
        ar.seek(self.xln_end)
        NHB = 11 if first else 12
        hid = ar.alloc_n("hid", NHB, [128, T], BF16)
        qwg = [ar.alloc_n("qw%dg" % (w + 1), 2, [128, D], BF16) for w in range(2)]
        w3f = [ar.alloc_n("w3f%d" % i, 11, [128, 512], BF16) for i in range(2)]
        if first:
            stg = ar.alloc_n("fst", 3, [128, 512], F32)
            abuf = ar.alloc_n("fab", 1, [128, 512], BF16)
            w3o = ar.alloc_n("fw3o", 1, [128, 512], BF16)
        qcnt = [0]

        def prep(sl):
            if not first:
                for w in range(2):
                    qg, qgn = qwg[w][sl % 2]
                    dqw = self.d_qw1 if w == 0 else self.d_qw2
                    p.dma("qd" if w == 0 else "gd", qg[:, :],
                          dqw.ap()[128 * sl:128 * (sl + 1), :],
                          r=["qw%ds_%d" % (w + 1, sl)], w=[qgn], tag=u + "wld2")
                return
            for w in range(3):
                srcw = (self.i_w1t, self.i_w2t, self.i_w3t)[w]
                dqn = ("qw1s_%d" % sl, "qw2s_%d" % sl, "qw3t_%d" % sl)[w]
                qg, qgn = qwg[w][sl % 2] if w < 2 else (None, None)
                for q in range(4):
                    sg, sgn_ = stg[qcnt[0] % 3]
                    qcnt[0] += 1
                    p.dma("gd" if w == 1 else "qd", sg[:, :],
                          srcw[128 * sl:128 * (sl + 1), 512 * q:512 * (q + 1)],
                          w=[sgn_], tag=u + "wld")
                    ab, abn = abuf[0]
                    p.do("se", lambda e, sg=sg, ab=ab, w=w: e.activation(
                            out=ab[:, :], in_=sg[:, :], func=AF.Sign,
                            bias=self.colC[:, 8 + w:9 + w]),
                         r=[sgn_, "colC"], w=[abn], tag=u + "sga")
                    if w < 2:
                        dst, dstn = qg[:, 512 * q:512 * (q + 1)], qgn
                    else:
                        ob, dstn = w3o[0]
                        dst = ob[:, :]
                    p.do("se", lambda e, sg=sg, dst=dst, w=w: e.activation(
                            out=dst, in_=sg[:, :], func=AF.Sign,
                            bias=self.colC[:, 4 + w:5 + w]),
                         r=[sgn_, "colC"], w=[dstn], tag=u + "sgn")
                    p.do("ve", lambda e, dst=dst, ab=ab: e.tensor_add(
                            out=dst, in0=dst, in1=ab[:, :]),
                         r=[dstn, abn], w=[dstn], tag=u + "q2")
                    if w == 2:
                        p.dma("gd", self.d_qw3.ap()[128 * sl:128 * (sl + 1),
                                                    512 * q:512 * (q + 1)],
                              dst, r=[dstn], w=[dqn], tag=u + "w3st")
                if w < 2:
                    dqw = self.d_qw1 if w == 0 else self.d_qw2
                    p.dma("gd", dqw.ap()[128 * sl:128 * (sl + 1), :], qg[:, :],
                          r=[qgn], w=[dqn], tag=u + "wst")

        def flush(g):
            for r4 in range(4):
                pool = w3f[r4 % 2]
                for i in range(11):
                    wf, wfn = pool[i]
                    p.dma("qd", wf[:, :],
                          self.d_qw3.ap()[128 * (11 * g + i):128 * (11 * g + i + 1),
                                          512 * r4:512 * (r4 + 1)],
                          r=["qw3t_%d" % (11 * g + i)], w=[wfn], tag=u + "w3f")
                for dsl in range(4):
                    ds = 4 * r4 + dsl
                    for i in range(11):
                        wf, wfn = pool[i]
                        ht2, ht2n = hid[(11 * g + i) % NHB]
                        for t in range(2):
                            p.do("te", lambda e, wf=wf, dsl=dsl, ht2=ht2, t=t, i=i: e.matmul(
                                    self.pw[0][:, 512 * t:512 * (t + 1)],
                                    wf[:, 128 * dsl:128 * (dsl + 1)],
                                    ht2[:, 512 * t:512 * (t + 1)],
                                    start=(i == 0), stop=(i == 10)),
                                 r=[wfn, ht2n], w=["pw0"], tag=u + "w3mm")
                    p.do("ve", lambda e, ds=ds: e.scalar_tensor_tensor(
                            out=self.stream[ds][:, :], in0=self.pw[0][:, :], scalar=0.5,
                            in1=self.stream[ds][:, :], op0=ALU.mult, op1=ALU.add),
                         r=["pw0", "st%d" % ds], w=["st%d" % ds], tag=u + "fadd")

        prep(0)
        for sl in range(NH):
            if sl + 1 < NH:
                prep(sl + 1)
            qg1, qg1n = qwg[0][sl % 2]
            qg2, qg2n = qwg[1][sl % 2]
            for t in range(2):
                for c in range(ND):
                    x1ps = self.pb[t][:, :]
                    x1r = "pb%d" % t
                    p.do("te", lambda e, c=c, t=t, qg1=qg1, x1ps=x1ps: e.matmul(
                            x1ps, qg1[:, 128 * c:128 * (c + 1)],
                            xln[c][0][:, 512 * t:512 * (t + 1)],
                            start=(c == 0), stop=(c == ND - 1)),
                         r=[qg1n, xr[c]], w=[x1r], tag=u + "x1")
                    x2ps = self.pb[2][:, :] if t == 0 else self.pw[1][:, 0:512]
                    x2r = "pb2" if t == 0 else "pw1"
                    p.do("te", lambda e, c=c, t=t, qg2=qg2, x2ps=x2ps: e.matmul(
                            x2ps, qg2[:, 128 * c:128 * (c + 1)],
                            xln[c][0][:, 512 * t:512 * (t + 1)],
                            start=(c == 0), stop=(c == ND - 1)),
                         r=[qg2n, xr[c]], w=[x2r], tag=u + "x2")
            ht, htn = hid[sl % NHB]
            for t in range(2):
                x1r = "pb%d" % t
                x1ps = self.pb[t][:, :]
                x2ps = self.pb[2][:, :] if t == 0 else self.pw[1][:, 0:512]
                x2r = "pb2" if t == 0 else "pw1"
                p.do("se", lambda e, ht=ht, t=t, x1ps=x1ps: e.activation(
                        out=ht[:, 512 * t:512 * (t + 1)], in_=x1ps,
                        func=AF.Sigmoid, scale=0.5),
                     r=[x1r], w=[htn], tag=u + "sig")
                p.do("ve", lambda e, ht=ht, x2ps=x2ps, t=t: e.scalar_tensor_tensor(
                        out=ht[:, 512 * t:512 * (t + 1)],
                        in0=ht[:, 512 * t:512 * (t + 1)], scalar=0.5,
                        in1=x2ps, op0=ALU.mult, op1=ALU.mult),
                     r=[htn, x2r], w=[htn], tag=u + "hid")
            if sl % 11 == 10:
                flush(sl // 11)

    def attn_a(self, xln):
        p, cst, ar = self.p, self.cst, self.ar
        xr = [x[1] for x in xln]

        ar.seek(self.xln_end)
        sk_own = ar.alloc_n("skown", KV, [128, T], BF16)
        ktil = ar.alloc_n("ktil", KV, [128, T], BF16)
        expb = ar.alloc_n("expb", KV, [128, T], BF16)
        vt_own = ar.alloc_n("vtown", NT, [128, 512], BF16)
        trans_base = ar.pos()
        wfp = ar.alloc_n("wfp", 2, [128, 512], F32)
        wbf = ar.alloc_n("wbf", 16, [128, 512], BF16)
        kraw = ar.alloc_n("kraw", 1, [128, T], BF16)
        glb = ar.alloc("glb", [128, T], F32)
        ekp = ar.alloc_n("ekp", 3, [128, T], BF16)
        khat = ar.alloc_n("khat", 2, [128, 128], BF16)
        qev = ar.alloc_n("qev", 2, [128, 512], BF16)

        def load_wgroup(src, cols, tag):
            for c in range(ND):
                wf, wfn = wfp[c % 2]
                wb, wbn = wbf[c]
                p.dma("qd" if c % 2 == 0 else "gd", wf[:, :],
                      src[128 * c:128 * (c + 1), cols[0]:cols[1]],
                      w=[wfn], tag="ldw" + tag)
                p.do("ve", lambda e, wf=wf, wb=wb: e.tensor_copy(out=wb[:, :], in_=wf[:, :]),
                     r=[wfn], w=[wbn], tag="cw" + tag)

        R = lambda i: self.rowf[0:1, T * i:T * (i + 1)]
        RB = lambda i: self.rowb[0:1, T * i:T * (i + 1)]

        load_wgroup(self.i_wk, (0, 512), "k")
        for s in range(KV):
            kr, krn = kraw[0]
            for t in range(2):
                for c in range(ND):
                    wb, wbn = wbf[c]
                    p.do("te", lambda e, s=s, t=t, c=c, wb=wb: e.matmul(
                            self.pw[0][:, 512 * t:512 * (t + 1)], wb[:, 128 * s:128 * (s + 1)],
                            xln[c][0][:, 512 * t:512 * (t + 1)],
                            start=(c == 0), stop=(c == ND - 1)),
                         r=[wbn, xr[c]], w=["pw0"], tag="kproj")
            p.do("se", lambda e, kr=kr: e.activation(out=kr[:, :], in_=self.pw[0][:, :], func=AF.Copy),
                 r=["pw0"], w=[krn], tag="kevac")
            self.kproc(s, kr, krn, sk_own, ktil, expb, glb, ekp)

        load_wgroup(self.i_wv, (0, 512), "v")
        for tc in range(NT):
            for c in range(ND):
                wb, wbn = wbf[c]
                p.do("te", lambda e, tc=tc, c=c, wb=wb: e.matmul(
                        self.pb[tc % 2][:, :], xln[c][0][:, 128 * tc:128 * (tc + 1)], wb[:, :],
                        start=(c == 0), stop=(c == ND - 1)),
                     r=[wbn, xr[c]], w=["pb%d" % (tc % 2)], tag="vproj")
            vt, vtn = vt_own[tc]
            p.do("se", lambda e, vt=vt, tc=tc: e.activation(out=vt[:, :], in_=self.pb[tc % 2][:, :], func=AF.Copy),
                 r=["pb%d" % (tc % 2)], w=[vtn], tag="vevac")

        for s in range(KV):
            p.do("ve", lambda e, s=s: e.memset(self.S_f[s][:, :], 0.0), w=["Sf%d" % s], tag="S0")
            p.do("ve", lambda e, s=s: e.memset(self.S_b[s][0][:, :], 0.0), w=["Sb%d_0" % s], tag="Sb0")
        for c in range(NT):
            for s in range(KV):
                self.chain_step(s, c, ktil, khat, ekp, vt_own)

        for s in range(KV):
            p.dma("gd", self.d_akv_in[128 * s:128 * (s + 1), :], sk_own[s][0][:, :],
                  r=[sk_own[s][1]], w=["akv_in"], tag="ccp")
        for tc in range(NT):
            dst = self.d_akv_in[512 + 64 * tc:512 + 64 * (tc + 1), :].rearrange(
                "r (a b) -> (r a) b", a=2)
            p.dma("gd", dst, vt_own[tc][0][:, :], r=[vt_own[tc][1]], w=["akv_in"], tag="ccp")
        cc1 = p.cc("AllGather", ALU.bypass, [[0, 1], [2, 3], [4, 5], [6, 7]],
                   [self.d_akv_in.ap()], [self.d_akv_out.ap()],
                   r=["akv_in"], w=["akv_out"], tag="cc1")

        for s in range(KV):
            p.dma("gd", self.d_as_in[128 * s:128 * (s + 1), :], self.S_f[s][:, :],
                  r=["Sf%d" % s], w=["as_in"], tag="ccs")
        cc2 = p.cc("AllGather", ALU.bypass, [[0, 1], [2, 3], [4, 5], [6, 7]],
                   [self.d_as_in.ap()], [self.d_as_out.ap()],
                   r=["as_in"], w=["as_out"], tag="cc2")

        for g in range(4):
            load_wgroup(self.i_wq, (512 * g, 512 * (g + 1)), "q")
            for sl in range(4):
                dq = 4 * g + sl
                pair = self.pjp(sl)
                for t in range(2):
                    pap, prs = pair[t]
                    for c in range(ND):
                        wb, wbn = wbf[c]
                        p.do("te", lambda e, sl=sl, t=t, c=c, wb=wb, pap=pap: e.matmul(
                                pap, wb[:, 128 * sl:128 * (sl + 1)],
                                xln[c][0][:, 512 * t:512 * (t + 1)],
                                start=(c == 0), stop=(c == ND - 1)),
                             r=[wbn, xr[c]], w=[prs], tag="qproj")
                for t in range(2):
                    pap, prs = pair[t]
                    qe, qen = qev[t]
                    p.do("se", lambda e, qe=qe, pap=pap: e.activation(
                            out=qe[:, :], in_=pap, func=AF.Copy),
                         r=[prs], w=[qen], tag="qevac")
                    p.dma("gd", self.d_qstage[128 * dq:128 * (dq + 1), 512 * t:512 * (t + 1)],
                          qe[:, :], r=[qen], w=["qst%d" % dq], tag="qstg")
        self.tap_dram("a_q", self.d_qstage, [D, T], ["qst%d" % i for i in range(16)])
        self.tap_big("a_sk", sk_own)
        self.tap_big("a_ktil", ktil)

        sk_par = []
        vt_par = []
        ar.seek(trans_base)
        for s in range(KV):
            tpar = ar.alloc("skpar%d" % s, [128, T], BF16)
            sk_par.append(tpar)
            p.dma("qd", tpar[0][:, :], self.d_akv_out[128 * s:128 * (s + 1), :],
                  r=["akv_out"], w=[tpar[1]], tag="ldskp")
        for tc in range(NT):
            tpar = ar.alloc("vtpar%d" % tc, [128, 512], BF16)
            vt_par.append(tpar)
            src = self.d_akv_out[512 + 64 * tc:512 + 64 * (tc + 1), :].rearrange(
                "r (a b) -> (r a) b", a=2)
            p.dma("qd", tpar[0][:, :], src, r=["akv_out"], w=[tpar[1]], tag="ldvp")
        for s in range(KV):
            p.dma("qd", self.S_in[s][:, :], self.d_as_out[128 * s:128 * (s + 1), :],
                  r=["as_out"], w=["Sin%d" % s], tag="ldsin")

        for s in range(KV):
            for c in range(NT):
                p.do("ve", lambda e, s=s, c=c: e.tensor_scalar(
                        out=self.S_tmp[:, :], in0=self.S_in[s][:, :],
                        scalar1=self.colB[:, 8 * s + c:8 * s + c + 1], scalar2=None, op0=ALU.mult),
                     r=["Sin%d" % s, "colB"], w=["Stmp"], tag="seff1")
                p.do("ve", lambda e, s=s, c=c: e.tensor_add(
                        out=self.S_b[s][c][:, :], in0=self.S_tmp[:, :], in1=self.S_b[s][c][:, :]),
                     r=["Stmp", "Sb%d_%d" % (s, c)], w=["Sb%d_%d" % (s, c)], tag="seff2")

        hb = ar.alloc_n("hb", 4, [128, T], BF16)
        oga = ar.alloc_n("oga", 2, [128, T], BF16)
        hs = ar.alloc_n("hs", 2, [128, 128], BF16)
        ep = ar.alloc_n("ep", 3, [128, 512], BF16)
        ysb = ar.alloc("ysb", [128, 512], BF16)
        ar.seek(self.ar.base)
        obig = [ar.alloc("obig%d" % h2, [128, T], BF16) for h2 in range(H)]
        self.pending = []
        self._aa = dict(hb=hb, oga=oga, hs=hs, ep=ep, ysb=ysb, obig=obig,
                        sk_own=sk_own, sk_par=sk_par, ktil=ktil, expb=expb,
                        vt_own=vt_own, vt_par=vt_par, R=R, RB=RB)
        self.front_a(0)
        self.gla_a(0)
        for h in range(H):
            self.quad_a(h, 0)
            if h + 1 < H:
                self.front_a(h + 1)
            self.quad_a(h, 1)
            if h + 1 < H:
                self.gla_a(h + 1)
        while self.pending:
            self.emit_pending()
        self.obig = obig
        self.tap_big("a_o", obig)

        wofp = ar.alloc_n("wofp", 2, [128, 512], F32)
        wob = ar.alloc_n("wob", 16, [128, 512], BF16)
        for g in range(4):
            for c in range(ND):
                wf, wfn = wofp[c % 2]
                wb, wbn = wob[c]
                p.dma("qd" if c % 2 == 0 else "gd", wf[:, :],
                      self.i_wo[128 * c:128 * (c + 1), 512 * g:512 * (g + 1)],
                      w=[wfn], tag="ldwo")
                p.do("ve", lambda e, wf=wf, wb=wb: e.tensor_copy(out=wb[:, :], in_=wf[:, :]),
                     r=[wfn], w=[wbn], tag="cwo")
            for dsl in range(4):
                ds = 4 * g + dsl
                pair = self.pjp(dsl)
                for t in range(2):
                    pap, prs = pair[t]
                    for hh in range(H):
                        wb, wbn = wob[hh]
                        p.do("te", lambda e, hh=hh, t=t, dsl=dsl, wb=wb, pap=pap: e.matmul(
                                pap, wb[:, 128 * dsl:128 * (dsl + 1)],
                                obig[hh][0][:, 512 * t:512 * (t + 1)],
                                start=(hh == 0), stop=(hh == H - 1)),
                             r=[wbn, obig[hh][1]], w=[prs], tag="wo")
                for t in range(2):
                    pap, prs = pair[t]
                    p.do("ve", lambda e, ds=ds, pap=pap, t=t: e.tensor_add(
                            out=self.stream[ds][:, 512 * t:512 * (t + 1)], in0=pap,
                            in1=self.stream[ds][:, 512 * t:512 * (t + 1)]),
                         r=[prs, "st%d" % ds], w=["st%d" % ds], tag="resadd")

    def kproc(self, s, kr, krn, sk_own, ktil, expb, glb, ekp):
        p, cst = self.p, self.cst
        R = lambda i: self.rowf[0:1, T * i:T * (i + 1)]
        RB = lambda i: self.rowb[0:1, T * i:T * (i + 1)]
        ra, ran = ekp[0]
        rb_, rbn = ekp[1]
        sk, skn = sk_own[s]
        p.do("se", lambda e: e.activation(out=rb_[0:64, :], in_=self.pw[0][64:128, :], func=AF.Copy),
             r=["pw0"], w=[rbn], tag="kswap1")
        p.do("se", lambda e: e.activation(out=rb_[64:128, :], in_=self.pw[0][0:64, :], func=AF.Copy),
             r=["pw0"], w=[rbn], tag="kswap2")
        p.do("ve", lambda e: e.tensor_mul(out=ra[:, :], in0=kr[:, :], in1=cst["rope_cos"][:, :]),
             r=[krn, "c_rope_cos"], w=[ran], tag="krope1")
        p.do("ve", lambda e: e.tensor_mul(out=rb_[:, :], in0=rb_[:, :], in1=cst["rope_sin"][:, :]),
             r=[rbn, "c_rope_sin"], w=[rbn], tag="krope2")
        p.do("ve", lambda e: e.tensor_add(out=sk[:, :], in0=ra[:, :], in1=rb_[:, :]),
             r=[ran, rbn], w=[skn], tag="krope3")
        ek, ekn = ekp[2]
        p.do("se", lambda e: e.activation(out=ek[:, :], in_=kr[:, :], func=AF.Exp),
             r=[krn], w=[ekn], tag="ksm1")
        for t in range(2):
            ps = self.pb[2][0:1, :] if t == 0 else self.pb[0][0:1, :]
            pr = "pb2" if t == 0 else "pb0"
            p.do("te", lambda e, t=t, ps=ps: e.matmul(
                    ps, cst["ones_col_bf"][:, :], ek[:, 512 * t:512 * (t + 1)],
                    start=True, stop=True),
                 r=[ekn, "c_ones_col_bf"], w=[pr], tag="ksm2")
            p.do("ve", lambda e, t=t, ps=ps: e.reciprocal(out=R(0)[:, 512 * t:512 * (t + 1)], in_=ps),
                 r=[pr], w=["rowS0"], tag="ksm3")
        p.do("ve", lambda e: e.tensor_copy(out=RB(0)[:, :], in_=R(0)[:, :]),
             r=["rowS0"], w=["rowB0"], tag="ksm4")
        ks, ksn = ekp[0]
        for t in range(2):
            bps = self.pw[1][:, :] if t == 0 else self.pb[2][:, :]
            bpr = "pw1" if t == 0 else "pb2"
            p.do("te", lambda e, t=t, bps=bps: e.matmul(
                    bps, cst["ones_row_bf"][:, :],
                    RB(0)[:, 512 * t:512 * (t + 1)], start=True, stop=True),
                 r=["rowB0", "c_ones_row_bf"], w=[bpr], tag="ksm5")
            p.do("ve", lambda e, t=t, ks=ks, bps=bps: e.tensor_mul(
                    out=ks[:, 512 * t:512 * (t + 1)], in0=ek[:, 512 * t:512 * (t + 1)], in1=bps),
                 r=[ekn, bpr], w=[ksn], tag="ksm6")
        p.do("se", lambda e: e.activation(out=rb_[:, :], in_=kr[:, :], func=AF.Abs),
             r=[krn], w=[rbn], tag="gl1")
        p.do("se", lambda e: e.activation(out=rb_[:, :], in_=rb_[:, :], func=AF.Exp, scale=-1.0),
             r=[rbn], w=[rbn], tag="gl2")
        p.do("se", lambda e: e.activation(out=glb[0][:, :], in_=rb_[:, :],
                                          func=AF.Ln, bias=cst["ones_col_f32"][:, :]),
             r=[rbn, "c_ones_col_f32"], w=[glb[1]], tag="gl3")
        p.do("ve", lambda e: e.tensor_scalar(out=rb_[:, :], in0=kr[:, :], scalar1=0.0,
                                             scalar2=None, op0=ALU.min),
             r=[krn], w=[rbn], tag="gl4")
        p.do("ve", lambda e: e.tensor_sub(out=glb[0][:, :], in0=rb_[:, :], in1=glb[0][:, :]),
             r=[rbn, glb[1]], w=[glb[1]], tag="gl5")
        for c in range(NT):
            p.do("ve", lambda e, c=c: e.tensor_tensor_scan(
                    out=glb[0][:, 128 * c:128 * (c + 1)], data0=glb[0][:, 128 * c:128 * (c + 1)],
                    data1=cst["zeros128"][:, :], initial=0.0, op0=ALU.add, op1=ALU.add),
                 r=[glb[1], "c_zeros128"], w=[glb[1]], tag="scan")
        eb, ebn = expb[s]
        p.do("se", lambda e, eb=eb: e.activation(out=eb[:, :], in_=glb[0][:, :], func=AF.Exp,
                                                 scale=1.0 / 16.0),
             r=[glb[1]], w=[ebn], tag="expb")
        enb, enbn = ekp[2]
        p.do("se", lambda e, enb=enb: e.activation(out=enb[:, :], in_=glb[0][:, :], func=AF.Exp,
                                                   scale=-1.0 / 16.0),
             r=[glb[1]], w=[enbn], tag="expnb")
        kt, ktn = ktil[s]
        p.do("ve", lambda e, kt=kt, ks=ks, enb=enb: e.tensor_mul(out=kt[:, :], in0=ks[:, :], in1=enb[:, :]),
             r=[ksn, enbn], w=[ktn], tag="ktil")
        for c in range(NT):
            p.do("se", lambda e, s=s, c=c: e.activation(
                    out=self.colA[:, 8 * s + c:8 * s + c + 1],
                    in_=glb[0][:, 128 * c + 127:128 * c + 128], func=AF.Exp, scale=1.0 / 16.0),
                 r=[glb[1]], w=["colA"], tag="bend")
        p.do("ve", lambda e, s=s: e.memset(self.colC[:, s:s + 1], 0.0),
             w=["colC"], tag="bcum0")
        for c in range(NT):
            p.do("se", lambda e, s=s, c=c: e.activation(
                    out=self.colB[:, 8 * s + c:8 * s + c + 1],
                    in_=self.colC[:, s:s + 1], func=AF.Exp, scale=1.0 / 16.0),
                 r=["colC"], w=["colB"], tag="bcum1")
            p.do("ve", lambda e, s=s, c=c: e.tensor_mul(
                    out=self.colB[:, 8 * s + c:8 * s + c + 1],
                    in0=self.colB[:, 8 * s + c:8 * s + c + 1], in1=cst["gate_col"][:, :]),
                 r=["colB", "c_gate_col"], w=["colB"], tag="bcum2")
            if c < NT - 1:
                p.do("ve", lambda e, s=s, c=c: e.tensor_add(
                        out=self.colC[:, s:s + 1], in0=self.colC[:, s:s + 1],
                        in1=glb[0][:, 128 * c + 127:128 * c + 128]),
                     r=["colC", glb[1]], w=["colC"], tag="bcum3")

    def chain_step(self, s, c, ktil, khat, ekp, vt_own):
        p, cst = self.p, self.cst
        kt, ktn = ktil[s]
        scr, scrn = ekp[1 + s % 2] if False else ekp[1]
        kh, khn = khat[s % 2]
        p.do("ve", lambda e, s=s, c=c, kt=kt, scr=scr: e.tensor_scalar(
                out=scr[:, 128 * (s % 2):128 * (s % 2) + 128], in0=kt[:, 128 * c:128 * (c + 1)],
                scalar1=self.colA[:, 8 * s + c:8 * s + c + 1], scalar2=None, op0=ALU.mult),
             r=[ktn, "colA"], w=[scrn], tag="khat1")
        p.do("te", lambda e, scr=scr, s=s: e.transpose(
                self.pbb[:, 128 * (s % 2):128 * (s % 2) + 128],
                scr[:, 128 * (s % 2):128 * (s % 2) + 128], cst["ident_bf16"][:, :]),
             r=[scrn, "c_ident_bf16"], w=["pbb"], tag="khat2")
        p.do("ve", lambda e, kh=kh, s=s: e.tensor_copy(
                out=kh[:, :], in_=self.pbb[:, 128 * (s % 2):128 * (s % 2) + 128]),
             r=["pbb"], w=[khn], tag="khat3")
        vt, vtn = vt_own[c]
        p.do("te", lambda e, kh=kh, vt=vt, s=s: e.matmul(
                self.pb[2][:, 0:128], kh[:, :], vt[:, 128 * s:128 * (s + 1)],
                start=True, stop=True),
             r=[khn, vtn], w=["pb2"], tag="chmm")
        p.do("ve", lambda e, s=s, c=c: e.tensor_scalar(
                out=self.S_f[s][:, :], in0=self.S_f[s][:, :],
                scalar1=self.colA[:, 8 * s + c:8 * s + c + 1], scalar2=None, op0=ALU.mult),
             r=["Sf%d" % s, "colA"], w=["Sf%d" % s], tag="chdec")
        p.do("ve", lambda e, s=s: e.tensor_add(out=self.S_f[s][:, :], in0=self.S_f[s][:, :],
                                               in1=self.pb[2][:, 0:128]),
             r=["Sf%d" % s, "pb2"], w=["Sf%d" % s], tag="chadd")
        if c < NT - 1:
            p.do("ve", lambda e, s=s, c=c: e.tensor_copy(out=self.S_b[s][c + 1][:, :],
                                                         in_=self.S_f[s][:, :]),
                 r=["Sf%d" % s], w=["Sb%d_%d" % (s, c + 1)], tag="chsnap")

    def front_a(self, h):
        a = self._aa
        p, cst = self.p, self.cst
        hb, R, RB = a["hb"], a["R"], a["RB"]
        s_ = h // 4
        qraw, qrn = hb[0]
        qsw, qswn = hb[1]
        sq, sqn = hb[2 + h % 2]
        ek, ekn = hb[1]
        qs, qsn = hb[0]
        p.dma("qd", qraw[:, :], self.d_qstage[128 * h:128 * (h + 1), :],
              r=["qst%d" % h], w=[qrn], tag="ldq")
        p.dma("qd", qsw[0:64, :], self.d_qstage[128 * h + 64:128 * (h + 1), :],
              r=["qst%d" % h], w=[qswn], tag="ldq2")
        p.dma("qd", qsw[64:128, :], self.d_qstage[128 * h:128 * h + 64, :],
              r=["qst%d" % h], w=[qswn], tag="ldq3")
        p.do("ve", lambda e, sq=sq: e.tensor_mul(out=sq[:, :], in0=qraw[:, :], in1=cst["rope_cos"][:, :]),
             r=[qrn, "c_rope_cos"], w=[sqn], tag="qrope1")
        p.do("ve", lambda e: e.tensor_mul(out=qsw[:, :], in0=qsw[:, :], in1=cst["rope_sin"][:, :]),
             r=[qswn, "c_rope_sin"], w=[qswn], tag="qrope2")
        p.do("ve", lambda e, sq=sq: e.tensor_add(out=sq[:, :], in0=sq[:, :], in1=qsw[:, :]),
             r=[sqn, qswn], w=[sqn], tag="qrope3")
        p.do("se", lambda e: e.activation(out=ek[:, :], in_=qraw[:, :], func=AF.Exp),
             r=[qrn], w=[ekn], tag="qsm1")
        for t in range(2):
            ps = self.pb[2][0:1, :] if t == 0 else self.pb[0][0:1, :]
            pr = "pb2" if t == 0 else "pb0"
            p.do("te", lambda e, t=t, ps=ps: e.matmul(
                    ps, cst["ones_col_bf"][:, :], ek[:, 512 * t:512 * (t + 1)],
                    start=True, stop=True),
                 r=[ekn, "c_ones_col_bf"], w=[pr], tag="qsm2")
            p.do("ve", lambda e, t=t, ps=ps: e.reciprocal(out=R(0)[:, 512 * t:512 * (t + 1)], in_=ps),
                 r=[pr], w=["rowS0"], tag="qsm3")
        p.do("ve", lambda e: e.tensor_copy(out=RB(0)[:, :], in_=R(0)[:, :]),
             r=["rowS0"], w=["rowB0"], tag="qsm4")
        for t in range(2):
            bps = self.pw[1][:, :] if t == 0 else self.pb[2][:, :]
            bpr = "pw1" if t == 0 else "pb2"
            p.do("te", lambda e, t=t, bps=bps: e.matmul(
                    bps, cst["ones_row_bf"][:, :],
                    RB(0)[:, 512 * t:512 * (t + 1)], start=True, stop=True),
                 r=["rowB0", "c_ones_row_bf"], w=[bpr], tag="qsm5")
            p.do("ve", lambda e, t=t, bps=bps, bpr=bpr: e.tensor_mul(
                    out=qs[:, 512 * t:512 * (t + 1)], in0=ek[:, 512 * t:512 * (t + 1)], in1=bps),
                 r=[ekn, bpr], w=[qsn], tag="qsm6")
        qt, qtn = hb[1]
        eb, ebn = a["expb"][s_]
        p.do("ve", lambda e, qt=qt, eb=eb: e.tensor_mul(out=qt[:, :], in0=qs[:, :], in1=eb[:, :]),
             r=[qsn, ebn], w=[qtn], tag="qtil")

    def gla_a(self, h):
        a = self._aa
        p, cst = self.p, self.cst
        s_ = h // 4
        oga = a["oga"][h % 2]
        hs = a["hs"]
        qt, qtn = a["hb"][1]
        kt, ktn = a["ktil"][s_]
        for c in range(NT):
            pt, ptn = hs[c % 2]
            p.do("te", lambda e, c=c, kt=kt, qt=qt: e.matmul(
                    self.pb[2][:, 0:128], kt[:, 128 * c:128 * (c + 1)], qt[:, 128 * c:128 * (c + 1)],
                    start=True, stop=True),
                 r=[ktn, qtn], w=["pb2"], tag="pt")
            p.do("ve", lambda e, pt=pt: e.tensor_mul(out=pt[:, :], in0=self.pb[2][:, 0:128],
                                                     in1=cst["tri_gla"][:, :]),
                 r=["pb2", "c_tri_gla"], w=[ptn], tag="ptmask")
            vt, vtn = a["vt_own"][c]
            p.do("te", lambda e, pt=pt, vt=vt, s_=s_: e.matmul(
                    self.pw[1][:, 0:128], vt[:, 128 * s_:128 * (s_ + 1)], pt[:, :],
                    start=True, stop=False),
                 r=[ptn, vtn], w=["pw1"], tag="oglam1")
            p.do("te", lambda e, s_=s_, c=c, qt=qt: e.matmul(
                    self.pw[1][:, 0:128], self.S_b[s_][c][:, :], qt[:, 128 * c:128 * (c + 1)],
                    start=False, stop=True),
                 r=["Sb%d_%d" % (s_, c), qtn], w=["pw1"], tag="oglam2")
            p.do("se", lambda e, c=c, oga=oga: e.activation(out=oga[0][:, 128 * c:128 * (c + 1)],
                                                   in_=self.pw[1][:, 0:128], func=AF.Copy, scale=0.5),
                 r=["pw1"], w=[oga[1]], tag="ogev")

    def quad_a(self, h, j):
        a = self._aa
        p, cst = self.p, self.cst
        s_ = h // 4
        sq, sqn = a["hb"][2 + h % 2]
        ep, R, RB = a["ep"], a["R"], a["RB"]
        sk_own, sk_par = a["sk_own"], a["sk_par"]
        vt_own, vt_par = a["vt_own"], a["vt_par"]
        sp = (2 * h + j) % 2
        sqsl = sq[:, 512 * j:512 * (j + 1)]

        def score(ck, j=j, sqsl=sqsl):
            if ck < 8:
                ksrc, ksrcn = sk_own[s_]
                koff = 128 * ck
            else:
                ksrc, ksrcn = sk_par[s_]
                koff = 128 * (ck - 8)
            pbi = ck % 2
            p.do("te", lambda e, ksrc=ksrc, koff=koff, sqsl=sqsl, pbi=pbi: e.matmul(
                    self.pb[pbi][:, :], ksrc[:, koff:koff + 128], sqsl,
                    start=True, stop=True),
                 r=[ksrcn, sqn], w=["pb%d" % pbi], tag="score")

        score(0)
        for ck in range(NKT):
            if ck + 1 < NKT:
                score(ck + 1)
            vsrc, vsrcn = vt_own[ck] if ck < 8 else vt_par[ck - 8]
            pbi = ck % 2
            band = (4 * j <= ck <= 4 * j + 3)
            epd, epdn = ep[ck % 3]
            bidx = 16 * j + ck
            p.do("se", lambda e, pbi=pbi, epd=epd, bidx=bidx: e.activation(
                    out=epd[:, :], in_=self.pb[pbi][:, :], func=AF.Exp,
                    scale=float(DK) ** -0.5, bias=cst["bias_tbl"][:, bidx:bidx + 1]),
                 r=["pb%d" % pbi, "c_bias_tbl"], w=[epdn], tag="expq")
            if band:
                i = ck - 4 * j
                p.do("ve", lambda e, epd=epd, i=i: e.tensor_mul(
                        out=epd[:, :], in0=epd[:, :],
                        in1=cst["band_masks"][:, 512 * i:512 * (i + 1)]),
                     r=[epdn, "c_band_masks"], w=[epdn], tag="bandm")
            p.do("te", lambda e, vsrc=vsrc, s_=s_, epd=epd, j=j, ck=ck: e.matmul(
                    self.pw[0][:, 512 * j:512 * (j + 1)], vsrc[:, 128 * s_:128 * (s_ + 1)],
                    epd[:, :], start=(ck == 0), stop=(ck == NKT - 1)),
                 r=[vsrcn, epdn], w=["pw0"], tag="ymm")
            p.do("te", lambda e, epd=epd, ck=ck: e.matmul(
                    self.pdn[0:1, :], cst["ones_col_bf"][:, :], epd[:, :],
                    start=(ck == 0), stop=(ck == NKT - 1)),
                 r=[epdn, "c_ones_col_bf"], w=["pdn"], tag="dnp")
        p.do("ve", lambda e, sp=sp: e.reciprocal(out=R(1 + sp)[:, 0:512], in_=self.pdn[0:1, :]),
             r=["pdn"], w=["rowS%d" % (1 + sp)], tag="dnrec")
        p.do("ve", lambda e, sp=sp: e.tensor_copy(out=RB(1)[:, 512 * sp:512 * (sp + 1)],
                                                  in_=R(1 + sp)[:, 0:512]),
             r=["rowS%d" % (1 + sp)], w=["rowB1"], tag="dnrecb")
        self.emit_pending()
        self.pending.append(self.fin_quad(h, j, sp, a["ysb"], a["oga"][h % 2],
                                          a["obig"][h], half=True))

    def fin_quad(self, h, j, sp, ysbp, oga, obig, half):
        p, cst = self.p, self.cst
        RB = lambda i: self.rowb[0:1, T * i:T * (i + 1)]
        ys, ysn = ysbp
        brow = cst["half_row_bf"] if half else cst["ones_row_bf"]
        brn = "c_half_row_bf" if half else "c_ones_row_bf"

        def emit():
            p.do("te", lambda e: e.matmul(self.pb[2][:, :], brow[:, :],
                                          RB(1)[:, 512 * sp:512 * (sp + 1)],
                                          start=True, stop=True),
                 r=["rowB1", brn], w=["pb2"], tag="dnbc")
            p.do("se", lambda e: e.activation(out=ys[:, :], in_=self.pw[0][:, 512 * j:512 * (j + 1)],
                                              func=AF.Copy),
                 r=["pw0"], w=[ysn], tag="yev")
            p.do("ve", lambda e: e.tensor_mul(out=ys[:, :], in0=ys[:, :], in1=self.pb[2][:, :]),
                 r=[ysn, "pb2"], w=[ysn], tag="ynorm")
            if oga is not None:
                p.do("ve", lambda e: e.tensor_add(out=obig[0][:, 512 * j:512 * (j + 1)],
                                                  in0=ys[:, :], in1=oga[0][:, 512 * j:512 * (j + 1)]),
                     r=[ysn, oga[1]], w=[obig[1]], tag="comb")
            else:
                bv = self.bcols_v
                p.do("ve", lambda e: e.tensor_scalar(
                        out=obig[0][:, 512 * j:512 * (j + 1)], in0=ys[:, :],
                        scalar1=bv[0][:, h:h + 1], scalar2=None, op0=ALU.add),
                     r=[ysn, bv[1]], w=[obig[1]], tag="bcomb")
        return emit

    def emit_pending(self):
        if self.pending:
            self.pending.pop(0)()

    def tap_big(self, name, tiles, f32=False):
        if name not in self.taps:
            return
        p = self.p
        out = self.nc.dram_tensor("tap_" + name, [len(tiles) * 128, T], F32,
                                  kind="ExternalOutput").ap()
        for i, (t, rn) in enumerate(tiles):
            dma = p.dma("gd", out[128 * i:128 * (i + 1), :], t[:, :],
                        r=[rn], w=["tap_%s_%d" % (name, i)], tag="tapd")
            self.final_dmas.append(dma)

    def tap_dram(self, name, dram, shape, deps):
        if name not in self.taps:
            return
        out = self.nc.dram_tensor("tap_" + name, shape, F32, kind="ExternalOutput").ap()
        dma = self.p.dma("gd", out, dram.ap(), r=deps, w=["tap_" + name], tag="tapdd")
        self.final_dmas.append(dma)

    def layernorm(self, idx):
        p = self.p
        cst = self.cst
        ar = self.ar
        xln = [ar.alloc("xln", [128, T], BF16) for _ in range(ND)]
        self.xln_end = ar.pos()
        xbs = [ar.alloc("lnxb", [128, T], BF16) for _ in range(2)]
        x2s = [ar.alloc("lnx2", [128, T], BF16) for _ in range(2)]
        for c in range(ND):
            xb, xbn = xbs[c % 2]
            x2, x2n = x2s[c % 2]
            p.do("ve", lambda e, c=c, xb=xb: e.tensor_copy(out=xb[:, :], in_=self.stream[c][:, :]),
                 r=["st%d" % c], w=[xbn], tag="lncast")
            p.do("se", lambda e, xb=xb, x2=x2: e.activation(out=x2[:, :], in_=xb[:, :], func=AF.Square),
                 r=[xbn], w=[x2n], tag="lnsq")
            for t in range(2):
                p.do("te", lambda e, c=c, t=t, xb=xb: e.matmul(
                        self.pb[t][0:1, :], cst["ones_col_bf"][:, :], xb[:, 512 * t:512 * (t + 1)],
                        start=(c == 0), stop=(c == ND - 1)),
                     r=[xbn, "c_ones_col_bf"], w=["pb%d" % t], tag="lnsum")
                sqps = self.pb[2][0:1, :] if t == 0 else self.pw[1][0:1, 0:512]
                sqr = "pb2" if t == 0 else "pw1"
                x2sl = x2[:, 512 * t:512 * (t + 1)]
                p.do("te", lambda e, c=c, x2sl=x2sl, sqps=sqps: e.matmul(
                        sqps, cst["ones_col_bf"][:, :], x2sl,
                        start=(c == 0), stop=(c == ND - 1)),
                     r=[x2n, "c_ones_col_bf"], w=[sqr], tag="lnsq2")
        R = lambda i: self.rowf[0:1, T * i:T * (i + 1)]
        RB = lambda i: self.rowb[0:1, T * i:T * (i + 1)]
        for t in range(2):
            p.do("ve", lambda e, t=t: e.tensor_scalar(
                    out=R(0)[:, 512 * t:512 * (t + 1)], in0=self.pb[t][0:1, :],
                    scalar1=1.0 / D, scalar2=None, op0=ALU.mult),
                 r=["pb%d" % t], w=["rowS0"], tag="lnrow")
            sqps2 = self.pb[2][0:1, :] if t == 0 else self.pw[1][0:1, 0:512]
            sqr2 = "pb2" if t == 0 else "pw1"
            p.do("ve", lambda e, t=t, sqps2=sqps2: e.tensor_scalar(
                    out=R(1)[:, 512 * t:512 * (t + 1)], in0=sqps2,
                    scalar1=1.0 / D, scalar2=None, op0=ALU.mult),
                 r=[sqr2], w=["rowS1"], tag="lnrow2")
        p.do("se", lambda e: e.activation(out=R(2)[:, :], in_=R(0)[:, :], func=AF.Square),
             r=["rowS0"], w=["rowS2"], tag="lnm2")
        p.do("ve", lambda e: e.scalar_tensor_tensor(out=R(1)[:, :], in0=R(1)[:, :],
                scalar=float(LN_EPS), in1=R(2)[:, :], op0=ALU.add, op1=ALU.subtract),
             r=["rowS1", "rowS2"], w=["rowS1"], tag="lnvar")
        p.do("se", lambda e: e.activation(out=R(2)[:, :], in_=R(1)[:, :], func=AF.Ln),
             r=["rowS1"], w=["rowS2"], tag="lnln")
        p.do("se", lambda e: e.activation(out=R(1)[:, :], in_=R(2)[:, :], func=AF.Exp, scale=-0.5),
             r=["rowS2"], w=["rowS1"], tag="lnexp")
        p.do("ve", lambda e: e.tensor_mul(out=R(2)[:, :], in0=R(0)[:, :], in1=R(1)[:, :]),
             r=["rowS0", "rowS1"], w=["rowS2"], tag="lnmr")
        p.do("ve", lambda e: e.tensor_copy(out=RB(0)[:, :], in_=R(1)[:, :]),
             r=["rowS1"], w=["rowB0"], tag="lnrb")
        p.do("ve", lambda e: e.tensor_copy(out=RB(1)[:, :], in_=R(2)[:, :]),
             r=["rowS2"], w=["rowB1"], tag="lnmb")
        for t in range(2):
            p.do("te", lambda e, t=t: e.matmul(
                    self.pw[0][:, 512 * t:512 * (t + 1)], cst["ones_row_bf"][:, :],
                    RB(0)[:, 512 * t:512 * (t + 1)], start=True, stop=True),
                 r=["rowB0", "c_ones_row_bf"], w=["pw0"], tag="lnbc1")
            mrps = self.pw[1][:, :] if t == 0 else self.pdn[:, :]
            mrr = "pw1" if t == 0 else "pdn"
            p.do("te", lambda e, t=t, mrps=mrps: e.matmul(
                    mrps, cst["ones_row_bf"][:, :],
                    RB(1)[:, 512 * t:512 * (t + 1)], start=True, stop=True),
                 r=["rowB1", "c_ones_row_bf"], w=[mrr], tag="lnbc2")
        tmps = [ar.alloc("lntmp", [128, T], F32) for _ in range(2)]
        for c in range(ND):
            tmp, tmpn = tmps[c % 2]
            xt, xtn = xln[c]
            p.do("ve", lambda e, c=c, tmp=tmp: e.tensor_mul(out=tmp[:, :], in0=self.stream[c][:, :], in1=self.pw[0][:, :]),
                 r=["st%d" % c, "pw0"], w=[tmpn], tag="lnn1")
            for t in range(2):
                mrps = self.pw[1][:, :] if t == 0 else self.pdn[:, :]
                mrr = "pw1" if t == 0 else "pdn"
                p.do("ve", lambda e, t=t, tmp=tmp, xt=xt, mrps=mrps: e.tensor_sub(
                        out=xt[:, 512 * t:512 * (t + 1)], in0=tmp[:, 512 * t:512 * (t + 1)], in1=mrps),
                     r=[tmpn, mrr], w=[xtn], tag="lnn2")
        return xln


def make_in_maps(inputs):
    x = inputs["L0_attn_a_hidden_states"]
    base = {
        "wq": inputs["L0_attn_a_Wq"], "wk": inputs["L0_attn_a_Wk"],
        "wv": inputs["L0_attn_a_Wv"], "wo": inputs["L0_attn_a_Wo"],
        "w1t": np.ascontiguousarray(
            inputs["L0_ffn_w1"].reshape(NH, 128, 16, 128).transpose(0, 3, 2, 1).reshape(HID, D)),
        "w2t": np.ascontiguousarray(
            inputs["L0_ffn_w2"].reshape(NH, 128, 16, 128).transpose(0, 3, 2, 1).reshape(HID, D)),
        "w3t": np.ascontiguousarray(inputs["L0_ffn_w3"].T),
    }
    for k in "qkvo":
        base["codes_" + k] = inputs["L0_attn_b_w%s_codes" % k]
        base["scales_" + k] = inputs["L0_attn_b_w%s_scales" % k]
        base["bias_" + k] = np.ascontiguousarray(inputs["L0_attn_b_b" + k].reshape(16, 128).T)
    w1f = base["w1t"].reshape(-1)
    w2f = base["w2t"].reshape(-1)
    w3f = base["w3t"].reshape(-1)
    n8 = w1f.size // 8
    maps = []
    for core in range(8):
        b, half = core // 2, core % 2
        m = dict(base)
        m["x_fm"] = np.ascontiguousarray(x[b, 1024 * half:1024 * (half + 1), :].T)
        m["wsl0"] = np.ascontiguousarray(w1f[n8 * core:n8 * (core + 1)].reshape(128, 11264))
        m["wsl1"] = np.ascontiguousarray(w2f[n8 * core:n8 * (core + 1)].reshape(128, 11264))
        m["wsl2"] = np.ascontiguousarray(w3f[n8 * core:n8 * (core + 1)].reshape(128, 11264))
        m.update(host_constants(core))
        maps.append(m)
    return maps


def kernel(**inputs):
    bld = Build(taps=_TAPS)
    in_maps = make_in_maps(inputs)
    res = run_bass_kernel_spmd(bld.nc, in_maps, core_ids=list(range(8)))
    out = np.empty((4, 2048, 2048), dtype=np.float32)
    for core in range(8):
        b, half = core // 2, core % 2
        out[b, 1024 * half:1024 * (half + 1), :] = res.results[core]["y"].T
    if _TAPS:
        taps = {}
        for name in _TAPS:
            taps[name] = [res.results[c].get("tap_" + name) for c in range(8)]
        return out, taps
    return out


# version 14 (final)
# speedup vs baseline: 1.0886x; 1.0886x over previous
import numpy as np
import ml_dtypes
import concourse.bass as bass
import concourse.mybir as mybir
from concourse.bass_utils import run_bass_kernel_spmd

F32 = mybir.dt.float32
BF16 = mybir.dt.bfloat16
AF = mybir.ActivationFunctionType
ALU = mybir.AluOpType

T = 1024
NT = 8
D = 2048
ND = 16
H = 16
KV = 4
DK = 128
HID = 5632
NH = 44
SEQ = 2048
NKT = 16
LN_EPS = 1e-5
FFN_EPS = 1e-8
KILL = -1.0e5

_TAPS = []

BF = ml_dtypes.bfloat16


class Op:
    __slots__ = ("stream", "fn", "deps", "needed", "idx", "kind", "tag", "ccname")

    def __init__(self, stream, fn, idx, kind, tag):
        self.stream = stream
        self.fn = fn
        self.deps = []
        self.needed = False
        self.idx = idx
        self.kind = kind
        self.tag = tag


class P:
    STREAM_ENGINE = {"te": "tensor", "ve": "vector", "se": "scalar", "ge": "gpsimd",
                     "qd": "sync", "gd": "gpsimd", "sd": "scalar", "cc": "gpsimd"}

    def __init__(self, nc):
        self.nc = nc
        self.ops = []
        self.res = {}
        self.cc_n = 0

    def _entry(self, r):
        e = self.res.get(r)
        if e is None:
            e = self.res[r] = {"w": None, "r": [], "pre": []}
        return e

    def _touch(self, op, reads, writes):
        dep = lambda d: (d is not op and d not in op.deps) and op.deps.append(d)
        for r in reads:
            e = self._entry(r)
            if e["w"] is not None:
                dep(e["w"])
            for d in e["pre"]:
                dep(d)
            e["pre"] = []
            e["r"].append(op)
        for r in writes:
            e = self._entry(r)
            if e["w"] is not None:
                dep(e["w"])
            for d in e["r"]:
                dep(d)
            for d in e["pre"]:
                dep(d)
            self.res[r] = {"w": op, "r": [], "pre": []}

    def carry(self, new_res, old_ops):
        e = self._entry(new_res)
        e["pre"] = list(old_ops)

    def deps_of(self, res_names):
        out = []
        for r in res_names:
            e = self.res.get(r)
            if e is None:
                continue
            if e["w"] is not None:
                out.append(e["w"])
            out.extend(e["r"])
        return out

    def do(self, stream, fn, r=(), w=(), tag=""):
        op = Op(stream, fn, len(self.ops), "c", tag)
        self._touch(op, r, w)
        self.ops.append(op)
        return op

    def dma(self, stream, out, in_, r=(), w=(), tag=""):
        op = Op(stream, lambda e: e.dma_start(out=out, in_=in_), len(self.ops), "d", tag)
        self._touch(op, r, w)
        self.ops.append(op)
        return op

    def cc(self, kind, aluop, groups, ins, outs, r=(), w=(), tag=""):
        def fn(e):
            return e.collective_compute(kind, aluop, replica_groups=groups,
                                        ins=ins, outs=outs)
        op = Op("cc", fn, len(self.ops), "cc", tag)
        self._touch(op, r, w)
        self.ops.append(op)
        return op

    DMA_ROT = 8

    def emit(self, final_out_dmas=()):
        nc = self.nc
        for op in self.ops:
            if op.kind == "d":
                op.needed = True
            for d in op.deps:
                if d.kind == "c" and d.stream == op.stream == "te":
                    continue
                d.needed = True
        for op in final_out_dmas:
            op.needed = True
        counts = {}
        comp = {}
        throttle = {}
        for op in self.ops:
            if op.kind == "cc":
                self.cc_n += 1
                op.ccname = "cc%d" % self.cc_n
                comp[op] = (op.ccname, 1)
                continue
            if op.kind == "d":
                k = counts.get(op.stream, 0)
                counts[op.stream] = k + 1
                s = "%s%d" % (op.stream, k % self.DMA_ROT)
                comp[op] = (s, 16 * (k // self.DMA_ROT + 1))
                if k >= self.DMA_ROT:
                    throttle[op] = (s, 16 * (k // self.DMA_ROT))
                continue
            if op.needed:
                c = counts.get(op.stream, 0) + 1
                counts[op.stream] = c
                comp[op] = (op.stream, c)
            else:
                comp[op] = None
        sem_names = set()
        for v in comp.values():
            if v is not None:
                sem_names.add(v[0])
        import contextlib
        self.stack = contextlib.ExitStack()
        sems = {s: self.stack.enter_context(nc.semaphore("sem_" + s))
                for s in sorted(sem_names)}
        engines = {"tensor": nc.tensor, "vector": nc.vector, "scalar": nc.scalar,
                   "gpsimd": nc.gpsimd, "sync": nc.sync}
        known = {e: {} for e in engines}
        n_waits = 0
        with nc.Block():
            for op in self.ops:
                eng_name = self.STREAM_ENGINE[op.stream]
                eng = engines[eng_name]
                need = {}
                th = throttle.get(op)
                if th is not None:
                    need[th[0]] = max(need.get(th[0], 0), th[1])
                for dop in op.deps:
                    c = comp[dop]
                    if c is None:
                        continue
                    if dop.kind == "c" and dop.stream == op.stream == "te":
                        continue
                    s, v = c
                    need[s] = max(need.get(s, 0), v)
                for s, v in sorted(need.items()):
                    if known[eng_name].get(s, 0) >= v:
                        continue
                    eng.wait_ge(sems[s], v)
                    known[eng_name][s] = v
                    n_waits += 1
                inst = op.fn(eng)
                c = comp[op]
                if c is not None:
                    s, v = c
                    inst.then_inc(sems[s], 16 if op.kind == "d" else 1)
            fin = {}
            for op in final_out_dmas:
                s, v = comp[op]
                fin[s] = max(fin.get(s, 0), v)
            for s, v in sorted(fin.items()):
                nc.sync.wait_ge(sems[s], v)
        self.n_waits = n_waits


class Arena:

    def __init__(self, nc, p, base, top):
        self.nc, self.p = nc, p
        self.base, self.top = base, top
        self.ptr = base
        self.live = []
        self.uid = 0

    def pos(self):
        return self.ptr

    def seek(self, off):
        self.ptr = off

    def reset(self):
        self.ptr = self.base

    def alloc(self, name, shape, dtype):
        self.uid += 1
        name = "ov_%s_%d" % (name, self.uid)
        nb = int(np.prod(shape[1:])) * mybir.dt.size(dtype)
        nb = (nb + 31) // 32 * 32
        s, e = self.ptr, self.ptr + nb
        assert e <= self.top, "arena overflow at %s (%d + %d > %d)" % (name, s, nb, self.top)
        t = self.nc.alloc_sbuf_tensor_at(name, shape, dtype, offset=s)
        self.ptr = e
        killed = [r for r in self.live if r[0] < e and s < r[1]]
        if killed:
            self.p.carry(name, self.p.deps_of([r[2] for r in killed]))
            keep = [r for r in self.live if r not in killed]
            for (ks, ke, kn) in killed:
                if ks < s:
                    keep.append((ks, s, kn))
                if ke > e:
                    keep.append((e, ke, kn))
            self.live = keep
        self.live.append((s, e, name))
        return t, name

    def alloc_n(self, name, n, shape, dtype):
        return [self.alloc("%s%d" % (name, i), shape, dtype) for i in range(n)]


def host_constants(core_id):
    half = core_id % 2
    c = {}
    c["ident_bf16"] = np.eye(128, dtype=BF)
    c["ones_row_bf"] = np.ones((1, 128), dtype=BF)
    c["half_row_bf"] = np.full((1, 128), 0.5, dtype=BF)
    c["ones_col_f32"] = np.ones((128, 1), dtype=np.float32)
    c["ones_row_f32"] = np.ones((1, 128), dtype=np.float32)
    c["ones_col_bf"] = np.ones((128, 1), dtype=BF)
    s_i = np.arange(128)[:, None]
    t_i = np.arange(128)[None, :]
    c["tri_gla"] = (s_i <= t_i).astype(BF)
    bm = np.zeros((128, 4 * 512), dtype=BF)
    qq = np.arange(512)[None, :]
    for i in range(4):
        bm[:, 512 * i:512 * (i + 1)] = ((128 * i + s_i) <= qq).astype(BF)
    c["band_masks"] = bm
    bt = np.zeros((128, 32), dtype=np.float32)
    for j in range(2):
        for ck in range(16):
            if ck < 8:
                dead = ck > 4 * j + 3
            else:
                dead = (half == 0)
            if dead:
                bt[:, 16 * j + ck] = KILL
    c["bias_tbl"] = bt
    inv = 1.0 / 10000.0 ** (np.arange(0, 128, 2, dtype=np.float64) / 128)
    pos = np.arange(1024 * half, 1024 * half + 1024, dtype=np.float64)
    fr = pos[None, :] * inv[:, None]
    c["rope_cos"] = np.vstack([np.cos(fr), np.cos(fr)]).astype(BF)
    c["rope_sin"] = np.vstack([-np.sin(fr), np.sin(fr)]).astype(BF)
    c["gate_col"] = np.full((128, 1), float(half), dtype=np.float32)
    c["zeros128"] = np.zeros((128, 128), dtype=np.float32)
    bi = np.zeros((4, 128), dtype=BF)
    for r_ in range(4):
        bi[r_, 32 * r_:32 * (r_ + 1)] = 1
    c["blkind"] = bi
    sel = np.zeros((64, 2048), dtype=BF)
    for cc in range(16):
        for pp in range(128):
            sel[4 * cc + pp // 32, 128 * cc + pp] = 1
    c["selscale"] = sel
    return c


CONST_SPECS = [
    ("ident_bf16", [128, 128], BF16),
    ("ones_row_bf", [1, 128], BF16), ("half_row_bf", [1, 128], BF16),
    ("ones_col_f32", [128, 1], F32), ("ones_row_f32", [1, 128], F32), ("ones_col_bf", [128, 1], BF16),
    ("tri_gla", [128, 128], BF16), ("band_masks", [128, 2048], BF16),
    ("bias_tbl", [128, 32], F32),
    ("rope_cos", [128, 1024], BF16), ("rope_sin", [128, 1024], BF16),
    ("gate_col", [128, 1], F32), ("zeros128", [128, 128], F32),
    ("blkind", [4, 128], BF16), ("selscale", [64, 2048], BF16),
]


class Build:
    def __init__(self, taps=()):
        self.taps = list(taps)
        nc = self.nc = bass.Bass("TRN2", target_bir_lowering=False)
        self.p = P(nc)
        self.uid = 0
        self.tap_outs = {}
        self.final_dmas = []
        self.declare_io()
        self.alloc()
        self.body()
        self.p.emit(final_out_dmas=self.final_dmas)
        nc.finalize()

    def pjp(self, i):
        if i % 2 == 0:
            return [(self.pw[0][:, 0:512], "pw0"), (self.pw[0][:, 512:1024], "pw0")]
        return [(self.pw[1][:, :], "pw1"), (self.pdn[:, :], "pdn")]

    def declare_io(self):
        nc = self.nc
        dr = lambda n, s, dt=F32: nc.dram_tensor(n, s, dt, kind="ExternalInput").ap()
        self.i_x = dr("x_fm", [D, T])
        self.i_wq = dr("wq", [D, D])
        self.i_wk = dr("wk", [D, 512])
        self.i_wv = dr("wv", [D, 512])
        self.i_wo = dr("wo", [D, D])
        self.i_codes = {k: dr("codes_" + k, [D, D]) for k in "qkvo"}
        self.i_scales = {k: dr("scales_" + k, [64, D]) for k in "qkvo"}
        self.i_bias = {k: dr("bias_" + k, [128, 16]) for k in "qkvo"}
        self.i_w1t = dr("w1t", [HID, D])
        self.i_w2t = dr("w2t", [HID, D])
        self.i_w3t = dr("w3t", [HID, D])
        self.i_wsl = [dr("wsl%d" % i, [128, 11264]) for i in range(3)]
        self.consts = {n: dr(n, s, dt) for (n, s, dt) in CONST_SPECS}
        self.o_y = nc.dram_tensor("y", [D, T], F32, kind="ExternalOutput").ap()
        di = lambda n, s, dt: nc.dram_tensor(n, s, dt, kind="Internal")
        self.d_gamma_in = di("gamma_in", [1, 4], F32)
        self.d_gamma_out = di("gamma_out", [1, 4], F32)
        self.d_akv_in = di("akv_in", [1024, 1024], BF16)
        self.d_akv_out = di("akv_out", [2048, 1024], BF16)
        self.d_as_in = di("as_in", [512, 128], F32)
        self.d_as_out = di("as_out", [1024, 128], F32)
        self.d_bk_in = di("bk_in", [2048, 1024], BF16)
        self.d_bk_out = di("bk_out", [4096, 1024], BF16)
        self.d_bv_in = di("bv_in", [1024, 2048], BF16)
        self.d_bv_out = di("bv_out", [2048, 2048], BF16)
        self.d_qstage = di("qstage", [D, T], BF16)
        self.d_qw1 = di("qw1", [HID, D], BF16)
        self.d_qw2 = di("qw2", [HID, D], BF16)
        self.d_qw3 = di("qw3", [HID, D], BF16)

    def alloc(self):
        nc = self.nc
        sb = nc.alloc_sbuf_tensor
        self.stream = [sb("stream%d" % i, [128, T], F32) for i in range(ND)]
        self.rowf = sb("rowf", [1, 3 * T], F32)
        self.rowb = sb("rowb", [1, 2 * T], BF16)
        self.S_f = [sb("Sf%d" % i, [128, 128], F32) for i in range(KV)]
        self.S_tmp = sb("Stmp", [128, 128], F32)
        self.S_b = [[sb("Sb%d_%d" % (k, c), [128, 128], BF16) for c in range(NT)] for k in range(KV)]
        self.S_in = [sb("Sin%d" % k, [128, 128], F32) for k in range(KV)]
        self.colA = sb("colA", [128, 32], F32)
        self.colB = sb("colB", [128, 32], F32)
        self.colC = sb("colC", [128, 16], F32)
        self.gsc = sb("gsc", [1, 12], F32)
        self.bct = {k: sb("bct_" + k, [128, 16], F32) for k in "qkvo"}
        self.cst = {}
        for (n, s, dt) in CONST_SPECS:
            self.cst[n] = sb("c_" + n, s, dt)
        ps = nc.alloc_psum_tensor
        self.pb = [ps("pb%d" % i, [128, 512], F32) for i in range(3)]
        self.pbb = ps("pbb", [128, 1024], BF16)
        self.pw = [ps("pw0", [128, 1024], F32), ps("pw1", [128, 512], F32)]
        self.pdn = ps("pdn", [128, 512], F32)
        base = nc.sbuf_base
        top = nc.sbuf_top
        self.ar = Arena(nc, self.p, base, top)
        self.arena_kb = (top - base) // 1024

    def body(self):
        p = self.p
        for (n, s, dt) in CONST_SPECS:
            p.dma("qd", self.cst[n][:, :], self.consts[n], w=["c_" + n], tag="ldc")
        for i in range(ND):
            p.dma("qd", self.stream[i][:, :], self.i_x[128 * i:128 * (i + 1), :],
                  w=["st%d" % i], tag="ldx")
        self.ar.reset()
        self.gamma_prep()
        self.ar.reset()
        xln = self.layernorm(1)
        self.tap_big("ln1", xln)
        self.attn_a(xln)
        self.tap_big("x1", [(self.stream[i], "st%d" % i) for i in range(ND)], f32=True)
        self.ar.reset()
        xln = self.layernorm(2)
        self.ffn(xln, first=True)
        self.tap_big("x2", [(self.stream[i], "st%d" % i) for i in range(ND)], f32=True)
        self.ar.reset()
        xln = self.layernorm(3)
        self.attn_b(xln)
        self.tap_big("x3", [(self.stream[i], "st%d" % i) for i in range(ND)], f32=True)
        self.ar.reset()
        xln = self.layernorm(4)
        self.ffn(xln, first=False)
        for i in range(ND):
            dma = p.dma("qd", self.o_y[128 * i:128 * (i + 1), :], self.stream[i][:, :],
                        r=["st%d" % i], w=["out%d" % i], tag="sty")
            self.final_dmas.append(dma)

    def attn_b(self, xln):
        p, cst, ar = self.p, self.cst, self.ar
        xr = [x[1] for x in xln]
        ar.seek(self.xln_end)
        weff2 = [ar.alloc_n("weff%d" % i2, 16, [128, 512], BF16) for i2 in range(2)]
        cstg = ar.alloc_n("cstg", 2, [128, 512], F32)
        sstg = ar.alloc_n("sstg", 1, [64, D], F32)
        ssbf = ar.alloc_n("ssbf", 2, [64, D], BF16)
        qev = ar.alloc_n("bqev", 2, [128, T], BF16)
        bcols = {}
        for k in "qkvo":
            bt, btn = self.bct[k], "bct_" + k
            bcols[k] = (bt, btn)
            p.dma("qd", bt[:, :], self.i_bias[k], w=[btn], tag="bld")

        def dequant_group(key, cols, tag, pools=None, g=0):
            if pools is None:
                pools = (weff2[g % 2], cstg, sstg, ssbf)
            ret = pools[0]
            weff_, cstg_, sstg_, ssbf_ = pools
            ss, ssn = sstg_[0]
            sb_, sbn = ssbf_[0 if len(ssbf_) == 1 else ("qkvo".index(key) % 2)]
            if cols[0] == 0:
                p.dma("gd", ss[:, :], self.i_scales[key][:, :], w=[ssn], tag="sld" + tag)
                p.do("ve", lambda e, ss=ss, sb_=sb_: e.tensor_copy(out=sb_[:, :], in_=ss[:, :]),
                     r=[ssn], w=[sbn], tag="scst" + tag)
            for c in range(ND):
                cs, csn = cstg_[c % 2]
                wf, wfn = weff_[c]
                p.dma("qd" if c % 2 == 0 else "gd", cs[:, :],
                      self.i_codes[key][128 * c:128 * (c + 1), cols[0]:cols[1]],
                      w=[csn], tag="cld" + tag)
                p.do("te", lambda e, sb_=sb_, c=c, cols=cols: e.matmul(self.pb[2][:, :],
                        cst["selscale"][:, 128 * c:128 * (c + 1)],
                        sb_[:, cols[0]:cols[1]], start=True, stop=True),
                     r=[sbn, "c_selscale"], w=["pb2"], tag="sbc" + tag)
                p.do("ve", lambda e, cs=cs, wf=wf: e.tensor_mul(out=wf[:, :], in0=cs[:, :],
                                                                in1=self.pb[2][:, :]),
                     r=[csn, "pb2"], w=[wfn], tag="deq" + tag)
            return ret

        for g in range(4):
            weff = dequant_group("k", (512 * g, 512 * (g + 1)), "k", g=g)
            for sl4 in range(4):
                sl = 4 * g + sl4
                pair = self.pjp(sl4)
                for t in range(2):
                    pap, prs = pair[t]
                    for c in range(ND):
                        wf, wfn = weff[c]
                        p.do("te", lambda e, sl4=sl4, t=t, c=c, wf=wf, pap=pap: e.matmul(
                                pap, wf[:, 128 * sl4:128 * (sl4 + 1)],
                                xln[c][0][:, 512 * t:512 * (t + 1)],
                                start=(c == 0), stop=(c == ND - 1)),
                             r=[wfn, xr[c]], w=[prs], tag="bkproj")
                qe, qen = qev[sl4 % 2]
                for t in range(2):
                    pap, prs = pair[t]
                    p.do("se", lambda e, qe=qe, sl=sl, pap=pap, t=t: e.activation(
                            out=qe[:, 512 * t:512 * (t + 1)], in_=pap, func=AF.Identity,
                            bias=bcols["k"][0][:, sl:sl + 1]),
                         r=[prs, bcols["k"][1]], w=[qen], tag="bkevac")
                p.dma("gd", self.d_bk_in[128 * sl:128 * (sl + 1), :], qe[:, :],
                      r=[qen], w=["bk_in"], tag="bkst")

        vreg = self.d_bv_in.ap()
        for g in range(4):
            weff = dequant_group("v", (512 * g, 512 * (g + 1)), "v", g=g)
            for tc in range(NT):
                for c in range(ND):
                    wf, wfn = weff[c]
                    p.do("te", lambda e, tc=tc, c=c, wf=wf: e.matmul(
                            self.pb[tc % 2][:, :], xln[c][0][:, 128 * tc:128 * (tc + 1)], wf[:, :],
                            start=(c == 0), stop=(c == ND - 1)),
                         r=[wfn, xr[c]], w=["pb%d" % (tc % 2)], tag="bvproj")
                qe, qen = qev[tc % 2]
                p.do("se", lambda e, qe=qe, tc=tc: e.activation(
                        out=qe[:, 0:512], in_=self.pb[tc % 2][:, :], func=AF.Copy),
                     r=["pb%d" % (tc % 2)], w=[qen], tag="bvevac")
                dst = vreg[:, 512 * g:512 * (g + 1)][128 * tc:128 * (tc + 1), :]
                p.dma("gd", dst, qe[:, 0:512], r=[qen], w=["bv_in"], tag="bvst")

        ccb1 = p.cc("AllGather", ALU.bypass, [[0, 1], [2, 3], [4, 5], [6, 7]],
                    [self.d_bk_in.ap()], [self.d_bk_out.ap()],
                    r=["bk_in"], w=["bk_out"], tag="ccbk")
        ccb2 = p.cc("AllGather", ALU.bypass, [[0, 1], [2, 3], [4, 5], [6, 7]],
                    [self.d_bv_in.ap()], [self.d_bv_out.ap()],
                    r=["bv_in"], w=["bv_out"], tag="ccbv")

        for g in range(4):
            weff = dequant_group("q", (512 * g, 512 * (g + 1)), "q", g=g)
            for sl4 in range(4):
                sl = 4 * g + sl4
                pair = self.pjp(sl4)
                for t in range(2):
                    pap, prs = pair[t]
                    for c in range(ND):
                        wf, wfn = weff[c]
                        p.do("te", lambda e, sl4=sl4, t=t, c=c, wf=wf, pap=pap: e.matmul(
                                pap, wf[:, 128 * sl4:128 * (sl4 + 1)],
                                xln[c][0][:, 512 * t:512 * (t + 1)],
                                start=(c == 0), stop=(c == ND - 1)),
                             r=[wfn, xr[c]], w=[prs], tag="bqproj")
                qe, qen = qev[sl4 % 2]
                for t in range(2):
                    pap, prs = pair[t]
                    p.do("se", lambda e, qe=qe, sl=sl, pap=pap, t=t: e.activation(
                            out=qe[:, 512 * t:512 * (t + 1)], in_=pap, func=AF.Identity,
                            bias=bcols["q"][0][:, sl:sl + 1]),
                         r=[prs, bcols["q"][1]], w=[qen], tag="bqevac")
                p.dma("gd", self.d_qstage[128 * sl:128 * (sl + 1), :], qe[:, :],
                      r=[qen], w=["qst%d" % sl], tag="bqstg")

        ar.seek(self.xln_end)
        kbuf = ar.alloc_n("bk", 2, [128, SEQ], BF16)
        vbuf = ar.alloc_n("bv", 2, [128, SEQ], BF16)
        qbuf = ar.alloc_n("bq", 2, [128, T], BF16)
        ep = ar.alloc_n("bep", 3, [128, 512], BF16)
        ysb = ar.alloc("bysb", [128, 512], BF16)
        ar.seek(self.ar.base)
        obig = [ar.alloc("bobig%d" % h2, [128, T], BF16) for h2 in range(H)]
        self.bcols_v = bcols["v"]
        vflat = self.d_bv_out.ap()
        R = lambda i: self.rowf[0:1, T * i:T * (i + 1)]
        RB = lambda i: self.rowb[0:1, T * i:T * (i + 1)]
        self.pending = []
        for h in range(H):
            kb, kbn = kbuf[h % 2]
            vb, vbn = vbuf[h % 2]
            qb_, qbn = qbuf[h % 2]
            p.dma("qd", qb_[:, :], self.d_qstage[128 * h:128 * (h + 1), :],
                  r=["qst%d" % h], w=[qbn], tag="bldq")
            for r_ in range(2):
                p.dma("qd", kb[:, T * r_:T * (r_ + 1)],
                      self.d_bk_out[2048 * r_ + 128 * h:2048 * r_ + 128 * (h + 1), :],
                      r=["bk_out"], w=[kbn], tag="bldk")
                vsec = vflat[1024 * r_:1024 * (r_ + 1), 128 * h:128 * (h + 1)]
                src = vsec.rearrange("(tc pp) j -> pp tc j", pp=128)
                dstv = vb[:, 1024 * r_:1024 * (r_ + 1)].rearrange("p (tc j) -> p tc j", tc=8)
                p.dma("qd", dstv, src, r=["bv_out"], w=[vbn], tag="bldv")
            for j in range(2):
                sp = (2 * h + j) % 2
                qsl = qb_[:, 512 * j:512 * (j + 1)]

                def score(ck, kb=kb, kbn=kbn, qsl=qsl, qbn=qbn):
                    pbi = ck % 2
                    p.do("te", lambda e, kb=kb, ck=ck, qsl=qsl, pbi=pbi: e.matmul(
                            self.pb[pbi][:, :], kb[:, 128 * ck:128 * (ck + 1)], qsl,
                            start=True, stop=True),
                         r=[kbn, qbn], w=["pb%d" % pbi], tag="bscore")

                score(0)
                for ck in range(NKT):
                    if ck + 1 < NKT:
                        score(ck + 1)
                    pbi = ck % 2
                    epd, epdn = ep[ck % 3]
                    p.do("se", lambda e, pbi=pbi, epd=epd: e.activation(
                            out=epd[:, :], in_=self.pb[pbi][:, :], func=AF.Exp,
                            scale=float(DK) ** -0.5),
                         r=["pb%d" % pbi], w=[epdn], tag="bexp")
                    p.do("te", lambda e, vb=vb, ck=ck, epd=epd, j=j: e.matmul(
                            self.pw[0][:, 512 * j:512 * (j + 1)], vb[:, 128 * ck:128 * (ck + 1)],
                            epd[:, :], start=(ck == 0), stop=(ck == NKT - 1)),
                         r=[vbn, epdn], w=["pw0"], tag="bymm")
                    p.do("te", lambda e, epd=epd, ck=ck: e.matmul(
                            self.pdn[0:1, :], cst["ones_col_bf"][:, :], epd[:, :],
                            start=(ck == 0), stop=(ck == NKT - 1)),
                         r=[epdn, "c_ones_col_bf"], w=["pdn"], tag="bdnp")
                p.do("ve", lambda e, sp=sp: e.reciprocal(out=R(1 + sp)[:, 0:512], in_=self.pdn[0:1, :]),
                     r=["pdn"], w=["rowS%d" % (1 + sp)], tag="bdnrec")
                p.do("ve", lambda e, sp=sp: e.tensor_copy(out=RB(1)[:, 512 * sp:512 * (sp + 1)],
                                                          in_=R(1 + sp)[:, 0:512]),
                     r=["rowS%d" % (1 + sp)], w=["rowB1"], tag="bdnrecb")
                self.emit_pending()
                self.pending.append(self.fin_quad(h, j, sp, ysb, None, obig[h], half=False))
        while self.pending:
            self.emit_pending()
        self.tap_big("b_y", obig)

        wog = ar.alloc_n("wog", 16, [128, 512], BF16)
        cstg2 = ar.alloc_n("cstg2", 2, [128, 512], F32)
        sstg2 = ar.alloc_n("sstg2", 1, [64, D], F32)
        ssbf2 = ar.alloc_n("ssbf2", 1, [64, D], BF16)
        pools2 = (wog, cstg2, sstg2, ssbf2)
        for g in range(4):
            dequant_group("o", (512 * g, 512 * (g + 1)), "o", pools=pools2)
            for dsl in range(4):
                ds = 4 * g + dsl
                pair = self.pjp(dsl)
                for t in range(2):
                    pap, prs = pair[t]
                    for hh in range(H):
                        wf, wfn = wog[hh]
                        p.do("te", lambda e, hh=hh, t=t, dsl=dsl, wf=wf, pap=pap: e.matmul(
                                pap, wf[:, 128 * dsl:128 * (dsl + 1)],
                                obig[hh][0][:, 512 * t:512 * (t + 1)],
                                start=(hh == 0), stop=(hh == H - 1)),
                             r=[wfn, obig[hh][1]], w=[prs], tag="bwo")
                for t in range(2):
                    pap, prs = pair[t]
                    p.do("ve", lambda e, ds=ds, pap=pap, t=t: e.scalar_tensor_tensor(
                            out=self.stream[ds][:, 512 * t:512 * (t + 1)], in0=pap,
                            scalar=bcols["o"][0][:, ds:ds + 1],
                            in1=self.stream[ds][:, 512 * t:512 * (t + 1)],
                            op0=ALU.add, op1=ALU.add),
                         r=[prs, "st%d" % ds, bcols["o"][1]], w=["st%d" % ds], tag="bresadd")

    def gamma_prep(self):
        p, cst, ar = self.p, self.cst, self.ar
        ar.seek(self.ar.base + 88 * 1024)
        stg = ar.alloc_n("gstg", 2, [128, 1408], F32)
        CW = "colC"
        for w in range(3):
            for pc in range(8):
                sg, sgn = stg[pc % 2]
                p.dma("qd", sg[:, :], self.i_wsl[w][:, 1408 * pc:1408 * (pc + 1)],
                      w=[sgn], tag="gld")
                dst = self.colC[:, 8 + w:9 + w] if pc == 0 else self.colC[:, 11:12]
                p.do("ve", lambda e, sg=sg, dst=dst: e.reduce_sum(
                        out=dst, in_=sg[:, :], axis=mybir.AxisListType.X,
                        apply_absolute_value=True),
                     r=[sgn], w=[CW], tag="gred")
                if pc > 0:
                    p.do("ve", lambda e, w=w: e.tensor_add(
                            out=self.colC[:, 8 + w:9 + w], in0=self.colC[:, 8 + w:9 + w],
                            in1=self.colC[:, 11:12]),
                         r=[CW], w=[CW], tag="gacc")
            p.do("te", lambda e, w=w: e.matmul(self.pb[2][0:1, w:w + 1],
                    self.colC[:, 8 + w:9 + w], cst["ones_col_f32"][:, :],
                    start=True, stop=True),
                 r=[CW, "c_ones_col_f32"], w=["pb2"], tag="gmm")
        p.do("ve", lambda e: e.tensor_copy(out=self.gsc[0:1, 0:3], in_=self.pb[2][0:1, 0:3]),
             r=["pb2"], w=["gsc"], tag="gcp")
        p.do("ve", lambda e: e.memset(self.gsc[0:1, 3:4], 0.0), w=["gsc"], tag="gz")
        p.dma("gd", self.d_gamma_in[0:1, :], self.gsc[0:1, 0:4], r=["gsc"], w=["g_in"], tag="gst")
        p.cc("AllReduce", ALU.add, [[0, 1, 2, 3, 4, 5, 6, 7]],
             [self.d_gamma_in.ap()], [self.d_gamma_out.ap()],
             r=["g_in"], w=["g_out"], tag="cc0")
        p.dma("qd", self.gsc[0:1, 4:8], self.d_gamma_out[0:1, :], r=["g_out"], w=["gsc"], tag="gbk")
        NW = float(HID) * float(D)
        p.do("ve", lambda e: e.tensor_scalar(out=self.gsc[0:1, 8:11], in0=self.gsc[0:1, 4:7],
                scalar1=0.5 / NW, scalar2=0.5 * FFN_EPS, op0=ALU.mult, op1=ALU.add),
             r=["gsc"], w=["gsc"], tag="gth")
        p.do("ve", lambda e: e.tensor_scalar(out=self.gsc[0:1, 4:7], in0=self.gsc[0:1, 8:11],
                scalar1=-1.0, scalar2=None, op0=ALU.mult),
             r=["gsc"], w=["gsc"], tag="gneg")
        for w in range(3):
            p.do("te", lambda e, w=w: e.matmul(self.pb[2][:, 8 + w:9 + w],
                    cst["ones_row_f32"][:, :], self.gsc[0:1, 8 + w:9 + w],
                    start=True, stop=True),
                 r=["gsc", "c_ones_row_f32"], w=["pb2"], tag="gbc")
            p.do("ve", lambda e, w=w: e.tensor_copy(out=self.colC[:, 4 + w:5 + w],
                                                    in_=self.pb[2][:, 8 + w:9 + w]),
                 r=["pb2"], w=[CW], tag="gbcc")
            p.do("te", lambda e, w=w: e.matmul(self.pb[2][:, 12 + w:13 + w],
                    cst["ones_row_f32"][:, :], self.gsc[0:1, 4 + w:5 + w],
                    start=True, stop=True),
                 r=["gsc", "c_ones_row_f32"], w=["pb2"], tag="gbcn")
            p.do("ve", lambda e, w=w: e.tensor_copy(out=self.colC[:, 8 + w:9 + w],
                                                    in_=self.pb[2][:, 12 + w:13 + w]),
                 r=["pb2"], w=[CW], tag="gbccn")

    def ffn(self, xln, first):
        p, cst, ar = self.p, self.cst, self.ar
        xr = [x[1] for x in xln]
        u = "f1" if first else "f2"
        ar.seek(self.xln_end)
        NHB = 11 if first else 12
        hid = ar.alloc_n("hid", NHB, [128, T], BF16)
        qwg = [ar.alloc_n("qw%dg" % (w + 1), 2, [128, D], BF16) for w in range(2)]
        w3f = [ar.alloc_n("w3f%d" % i, 11, [128, 256], BF16) for i in range(2)]
        if first:
            stg = ar.alloc_n("fst", 3, [128, T], F32)
            abuf = ar.alloc_n("fab", 1, [128, T], BF16)
            w3o = ar.alloc_n("fw3o", 1, [128, T], BF16)
        qcnt = [0]

        def prep(sl):
            if not first:
                for w in range(2):
                    qg, qgn = qwg[w][sl % 2]
                    dqw = self.d_qw1 if w == 0 else self.d_qw2
                    p.dma("qd" if w == 0 else "gd", qg[:, :],
                          dqw.ap()[128 * sl:128 * (sl + 1), :],
                          r=["qw%ds_%d" % (w + 1, sl)], w=[qgn], tag=u + "wld2")
                return
            for w in range(3):
                srcw = (self.i_w1t, self.i_w2t, self.i_w3t)[w]
                dqn = ("qw1s_%d" % sl, "qw2s_%d" % sl, "qw3t_%d" % sl)[w]
                qg, qgn = qwg[w][sl % 2] if w < 2 else (None, None)
                for q in range(2):
                    sg, sgn_ = stg[qcnt[0] % 3]
                    qcnt[0] += 1
                    p.dma("gd" if w == 1 else "qd", sg[:, :],
                          srcw[128 * sl:128 * (sl + 1), T * q:T * (q + 1)],
                          w=[sgn_], tag=u + "wld")
                    ab, abn = abuf[0]
                    if w < 2:
                        dst, dstn = qg[:, T * q:T * (q + 1)], qgn
                    else:
                        ob, dstn = w3o[0]
                        dst = ob[:, :]
                    if w == 0:
                        p.do("se", lambda e, sg=sg, ab=ab, w=w: e.activation(
                                out=ab[:, :], in_=sg[:, :], func=AF.Sign,
                                bias=self.colC[:, 8 + w:9 + w]),
                             r=[sgn_, "colC"], w=[abn], tag=u + "sga")
                        p.do("se", lambda e, sg=sg, dst=dst, w=w: e.activation(
                                out=dst, in_=sg[:, :], func=AF.Sign,
                                bias=self.colC[:, 4 + w:5 + w]),
                             r=[sgn_, "colC"], w=[dstn], tag=u + "sgn")
                        p.do("ve", lambda e, dst=dst, ab=ab: e.tensor_add(
                                out=dst, in0=dst, in1=ab[:, :]),
                             r=[dstn, abn], w=[dstn], tag=u + "q2")
                    else:
                        p.do("ve", lambda e, sg=sg, ab=ab, w=w: e.tensor_scalar(
                                out=ab[:, :], in0=sg[:, :],
                                scalar1=self.colC[:, 4 + w:5 + w],
                                scalar2=None, op0=ALU.is_gt),
                             r=[sgn_, "colC"], w=[abn], tag=u + "wm")
                        p.do("ve", lambda e, sg=sg, dst=dst, w=w: e.tensor_scalar(
                                out=dst, in0=sg[:, :],
                                scalar1=self.colC[:, 8 + w:9 + w],
                                scalar2=None, op0=ALU.is_lt),
                             r=[sgn_, "colC"], w=[dstn], tag=u + "ws")
                        p.do("ve", lambda e, dst=dst, ab=ab: e.tensor_sub(
                                out=dst, in0=ab[:, :], in1=dst),
                             r=[dstn, abn], w=[dstn], tag=u + "wq")
                    if w == 2:
                        p.dma("gd", self.d_qw3.ap()[128 * sl:128 * (sl + 1),
                                                    T * q:T * (q + 1)],
                              dst, r=[dstn], w=[dqn], tag=u + "w3st")
                if w < 2:
                    dqw = self.d_qw1 if w == 0 else self.d_qw2
                    p.dma("gd", dqw.ap()[128 * sl:128 * (sl + 1), :], qg[:, :],
                          r=[qgn], w=[dqn], tag=u + "wst")

        def flush(g):
            for r8 in range(8):
                pool = w3f[r8 % 2]
                for i in range(11):
                    wf, wfn = pool[i]
                    p.dma("qd", wf[:, :],
                          self.d_qw3.ap()[128 * (11 * g + i):128 * (11 * g + i + 1),
                                          256 * r8:256 * (r8 + 1)],
                          r=["qw3t_%d" % (11 * g + i)], w=[wfn], tag=u + "w3f")
                for dsl in range(2):
                    ds = 2 * r8 + dsl
                    for i in range(11):
                        wf, wfn = pool[i]
                        ht2, ht2n = hid[(11 * g + i) % NHB]
                        for t in range(2):
                            p.do("te", lambda e, wf=wf, dsl=dsl, ht2=ht2, t=t, i=i: e.matmul(
                                    self.pw[0][:, 512 * t:512 * (t + 1)],
                                    wf[:, 128 * dsl:128 * (dsl + 1)],
                                    ht2[:, 512 * t:512 * (t + 1)],
                                    start=(i == 0), stop=(i == 10)),
                                 r=[wfn, ht2n], w=["pw0"], tag=u + "w3mm")
                    p.do("ve", lambda e, ds=ds: e.tensor_add(
                            out=self.stream[ds][:, :], in0=self.pw[0][:, :],
                            in1=self.stream[ds][:, :]),
                         r=["pw0", "st%d" % ds], w=["st%d" % ds], tag=u + "fadd")

        prep(0)
        for sl in range(NH):
            if sl + 1 < NH:
                prep(sl + 1)
            qg1, qg1n = qwg[0][sl % 2]
            qg2, qg2n = qwg[1][sl % 2]
            for t in range(2):
                for c in range(ND):
                    x1ps = self.pb[t][:, :]
                    x1r = "pb%d" % t
                    p.do("te", lambda e, c=c, t=t, qg1=qg1, x1ps=x1ps: e.matmul(
                            x1ps, qg1[:, 128 * c:128 * (c + 1)],
                            xln[c][0][:, 512 * t:512 * (t + 1)],
                            start=(c == 0), stop=(c == ND - 1)),
                         r=[qg1n, xr[c]], w=[x1r], tag=u + "x1")
                    x2ps = self.pb[2][:, :] if t == 0 else self.pw[1][:, 0:512]
                    x2r = "pb2" if t == 0 else "pw1"
                    p.do("te", lambda e, c=c, t=t, qg2=qg2, x2ps=x2ps: e.matmul(
                            x2ps, qg2[:, 128 * c:128 * (c + 1)],
                            xln[c][0][:, 512 * t:512 * (t + 1)],
                            start=(c == 0), stop=(c == ND - 1)),
                         r=[qg2n, xr[c]], w=[x2r], tag=u + "x2")
            ht, htn = hid[sl % NHB]
            for t in range(2):
                x1r = "pb%d" % t
                x1ps = self.pb[t][:, :]
                x2ps = self.pb[2][:, :] if t == 0 else self.pw[1][:, 0:512]
                x2r = "pb2" if t == 0 else "pw1"
                p.do("se", lambda e, ht=ht, t=t, x1ps=x1ps: e.activation(
                        out=ht[:, 512 * t:512 * (t + 1)], in_=x1ps,
                        func=AF.Sigmoid, scale=0.5),
                     r=[x1r], w=[htn], tag=u + "sig")
                p.do("ve", lambda e, ht=ht, x2ps=x2ps, t=t: e.tensor_mul(
                        out=ht[:, 512 * t:512 * (t + 1)],
                        in0=ht[:, 512 * t:512 * (t + 1)], in1=x2ps),
                     r=[htn, x2r], w=[htn], tag=u + "hid")
            if sl % 11 == 10:
                flush(sl // 11)

    def attn_a(self, xln):
        p, cst, ar = self.p, self.cst, self.ar
        xr = [x[1] for x in xln]

        ar.seek(self.xln_end)
        sk_own = ar.alloc_n("skown", KV, [128, T], BF16)
        ktil = ar.alloc_n("ktil", KV, [128, T], BF16)
        expb = ar.alloc_n("expb", KV, [128, T], BF16)
        vt_own = ar.alloc_n("vtown", NT, [128, 512], BF16)
        trans_base = ar.pos()
        wfp = ar.alloc_n("wfp", 2, [128, 512], F32)
        wbf = ar.alloc_n("wbf", 16, [128, 512], BF16)
        kraw = ar.alloc_n("kraw", 1, [128, T], BF16)
        glb = ar.alloc("glb", [128, T], F32)
        ekp = ar.alloc_n("ekp", 3, [128, T], BF16)
        khat = ar.alloc_n("khat", 2, [128, 128], BF16)
        qev = ar.alloc_n("qev", 2, [128, 512], BF16)

        def load_wgroup(src, cols, tag):
            for c in range(ND):
                wf, wfn = wfp[c % 2]
                wb, wbn = wbf[c]
                p.dma("qd" if c % 2 == 0 else "gd", wf[:, :],
                      src[128 * c:128 * (c + 1), cols[0]:cols[1]],
                      w=[wfn], tag="ldw" + tag)
                p.do("ve", lambda e, wf=wf, wb=wb: e.tensor_copy(out=wb[:, :], in_=wf[:, :]),
                     r=[wfn], w=[wbn], tag="cw" + tag)

        R = lambda i: self.rowf[0:1, T * i:T * (i + 1)]
        RB = lambda i: self.rowb[0:1, T * i:T * (i + 1)]

        load_wgroup(self.i_wk, (0, 512), "k")
        for s in range(KV):
            kr, krn = kraw[0]
            for t in range(2):
                for c in range(ND):
                    wb, wbn = wbf[c]
                    p.do("te", lambda e, s=s, t=t, c=c, wb=wb: e.matmul(
                            self.pw[0][:, 512 * t:512 * (t + 1)], wb[:, 128 * s:128 * (s + 1)],
                            xln[c][0][:, 512 * t:512 * (t + 1)],
                            start=(c == 0), stop=(c == ND - 1)),
                         r=[wbn, xr[c]], w=["pw0"], tag="kproj")
            p.do("se", lambda e, kr=kr: e.activation(out=kr[:, :], in_=self.pw[0][:, :], func=AF.Copy),
                 r=["pw0"], w=[krn], tag="kevac")
            self.kproc(s, kr, krn, sk_own, ktil, expb, glb, ekp)

        load_wgroup(self.i_wv, (0, 512), "v")
        for tc in range(NT):
            for c in range(ND):
                wb, wbn = wbf[c]
                p.do("te", lambda e, tc=tc, c=c, wb=wb: e.matmul(
                        self.pb[tc % 2][:, :], xln[c][0][:, 128 * tc:128 * (tc + 1)], wb[:, :],
                        start=(c == 0), stop=(c == ND - 1)),
                     r=[wbn, xr[c]], w=["pb%d" % (tc % 2)], tag="vproj")
            vt, vtn = vt_own[tc]
            p.do("se", lambda e, vt=vt, tc=tc: e.activation(out=vt[:, :], in_=self.pb[tc % 2][:, :], func=AF.Copy),
                 r=["pb%d" % (tc % 2)], w=[vtn], tag="vevac")

        for s in range(KV):
            p.do("ve", lambda e, s=s: e.memset(self.S_f[s][:, :], 0.0), w=["Sf%d" % s], tag="S0")
            p.do("ve", lambda e, s=s: e.memset(self.S_b[s][0][:, :], 0.0), w=["Sb%d_0" % s], tag="Sb0")
        for c in range(NT):
            for s in range(KV):
                self.chain_step(s, c, ktil, khat, ekp, vt_own)

        for s in range(KV):
            p.dma("gd", self.d_akv_in[128 * s:128 * (s + 1), :], sk_own[s][0][:, :],
                  r=[sk_own[s][1]], w=["akv_in"], tag="ccp")
        for tc in range(NT):
            dst = self.d_akv_in[512 + 64 * tc:512 + 64 * (tc + 1), :].rearrange(
                "r (a b) -> (r a) b", a=2)
            p.dma("gd", dst, vt_own[tc][0][:, :], r=[vt_own[tc][1]], w=["akv_in"], tag="ccp")
        cc1 = p.cc("AllGather", ALU.bypass, [[0, 1], [2, 3], [4, 5], [6, 7]],
                   [self.d_akv_in.ap()], [self.d_akv_out.ap()],
                   r=["akv_in"], w=["akv_out"], tag="cc1")

        for s in range(KV):
            p.dma("gd", self.d_as_in[128 * s:128 * (s + 1), :], self.S_f[s][:, :],
                  r=["Sf%d" % s], w=["as_in"], tag="ccs")
        cc2 = p.cc("AllGather", ALU.bypass, [[0, 1], [2, 3], [4, 5], [6, 7]],
                   [self.d_as_in.ap()], [self.d_as_out.ap()],
                   r=["as_in"], w=["as_out"], tag="cc2")

        for g in range(4):
            load_wgroup(self.i_wq, (512 * g, 512 * (g + 1)), "q")
            for sl in range(4):
                dq = 4 * g + sl
                pair = self.pjp(sl)
                for t in range(2):
                    pap, prs = pair[t]
                    for c in range(ND):
                        wb, wbn = wbf[c]
                        p.do("te", lambda e, sl=sl, t=t, c=c, wb=wb, pap=pap: e.matmul(
                                pap, wb[:, 128 * sl:128 * (sl + 1)],
                                xln[c][0][:, 512 * t:512 * (t + 1)],
                                start=(c == 0), stop=(c == ND - 1)),
                             r=[wbn, xr[c]], w=[prs], tag="qproj")
                for t in range(2):
                    pap, prs = pair[t]
                    qe, qen = qev[t]
                    p.do("se", lambda e, qe=qe, pap=pap: e.activation(
                            out=qe[:, :], in_=pap, func=AF.Copy),
                         r=[prs], w=[qen], tag="qevac")
                    p.dma("gd", self.d_qstage[128 * dq:128 * (dq + 1), 512 * t:512 * (t + 1)],
                          qe[:, :], r=[qen], w=["qst%d" % dq], tag="qstg")
        self.tap_dram("a_q", self.d_qstage, [D, T], ["qst%d" % i for i in range(16)])
        self.tap_big("a_sk", sk_own)
        self.tap_big("a_ktil", ktil)

        sk_par = []
        vt_par = []
        ar.seek(trans_base)
        for s in range(KV):
            tpar = ar.alloc("skpar%d" % s, [128, T], BF16)
            sk_par.append(tpar)
            p.dma("qd", tpar[0][:, :], self.d_akv_out[128 * s:128 * (s + 1), :],
                  r=["akv_out"], w=[tpar[1]], tag="ldskp")
        for tc in range(NT):
            tpar = ar.alloc("vtpar%d" % tc, [128, 512], BF16)
            vt_par.append(tpar)
            src = self.d_akv_out[512 + 64 * tc:512 + 64 * (tc + 1), :].rearrange(
                "r (a b) -> (r a) b", a=2)
            p.dma("qd", tpar[0][:, :], src, r=["akv_out"], w=[tpar[1]], tag="ldvp")
        for s in range(KV):
            p.dma("qd", self.S_in[s][:, :], self.d_as_out[128 * s:128 * (s + 1), :],
                  r=["as_out"], w=["Sin%d" % s], tag="ldsin")

        for s in range(KV):
            for c in range(NT):
                p.do("ve", lambda e, s=s, c=c: e.tensor_scalar(
                        out=self.S_tmp[:, :], in0=self.S_in[s][:, :],
                        scalar1=self.colB[:, 8 * s + c:8 * s + c + 1], scalar2=None, op0=ALU.mult),
                     r=["Sin%d" % s, "colB"], w=["Stmp"], tag="seff1")
                p.do("ve", lambda e, s=s, c=c: e.tensor_add(
                        out=self.S_b[s][c][:, :], in0=self.S_tmp[:, :], in1=self.S_b[s][c][:, :]),
                     r=["Stmp", "Sb%d_%d" % (s, c)], w=["Sb%d_%d" % (s, c)], tag="seff2")

        hb = ar.alloc_n("hb", 4, [128, T], BF16)
        oga = ar.alloc_n("oga", 2, [128, T], BF16)
        hs = ar.alloc_n("hs", 2, [128, 128], BF16)
        ep = ar.alloc_n("ep", 3, [128, 512], BF16)
        ysb = ar.alloc("ysb", [128, 512], BF16)
        ar.seek(self.ar.base)
        obig = [ar.alloc("obig%d" % h2, [128, T], BF16) for h2 in range(H)]
        self.pending = []
        self._aa = dict(hb=hb, oga=oga, hs=hs, ep=ep, ysb=ysb, obig=obig,
                        sk_own=sk_own, sk_par=sk_par, ktil=ktil, expb=expb,
                        vt_own=vt_own, vt_par=vt_par, R=R, RB=RB)
        self.front_a(0)
        self.gla_a(0)
        for h in range(H):
            self.quad_a(h, 0)
            if h + 1 < H:
                self.front_a(h + 1)
            self.quad_a(h, 1)
            if h + 1 < H:
                self.gla_a(h + 1)
        while self.pending:
            self.emit_pending()
        self.obig = obig
        self.tap_big("a_o", obig)

        wofp = ar.alloc_n("wofp", 2, [128, 512], F32)
        wob = ar.alloc_n("wob", 16, [128, 512], BF16)
        for g in range(4):
            for c in range(ND):
                wf, wfn = wofp[c % 2]
                wb, wbn = wob[c]
                p.dma("qd" if c % 2 == 0 else "gd", wf[:, :],
                      self.i_wo[128 * c:128 * (c + 1), 512 * g:512 * (g + 1)],
                      w=[wfn], tag="ldwo")
                p.do("ve", lambda e, wf=wf, wb=wb: e.tensor_copy(out=wb[:, :], in_=wf[:, :]),
                     r=[wfn], w=[wbn], tag="cwo")
            for dsl in range(4):
                ds = 4 * g + dsl
                pair = self.pjp(dsl)
                for t in range(2):
                    pap, prs = pair[t]
                    for hh in range(H):
                        wb, wbn = wob[hh]
                        p.do("te", lambda e, hh=hh, t=t, dsl=dsl, wb=wb, pap=pap: e.matmul(
                                pap, wb[:, 128 * dsl:128 * (dsl + 1)],
                                obig[hh][0][:, 512 * t:512 * (t + 1)],
                                start=(hh == 0), stop=(hh == H - 1)),
                             r=[wbn, obig[hh][1]], w=[prs], tag="wo")
                for t in range(2):
                    pap, prs = pair[t]
                    p.do("ve", lambda e, ds=ds, pap=pap, t=t: e.tensor_add(
                            out=self.stream[ds][:, 512 * t:512 * (t + 1)], in0=pap,
                            in1=self.stream[ds][:, 512 * t:512 * (t + 1)]),
                         r=[prs, "st%d" % ds], w=["st%d" % ds], tag="resadd")

    def kproc(self, s, kr, krn, sk_own, ktil, expb, glb, ekp):
        p, cst = self.p, self.cst
        R = lambda i: self.rowf[0:1, T * i:T * (i + 1)]
        RB = lambda i: self.rowb[0:1, T * i:T * (i + 1)]
        ra, ran = ekp[0]
        rb_, rbn = ekp[1]
        sk, skn = sk_own[s]
        p.do("se", lambda e: e.activation(out=rb_[0:64, :], in_=self.pw[0][64:128, :], func=AF.Copy),
             r=["pw0"], w=[rbn], tag="kswap1")
        p.do("se", lambda e: e.activation(out=rb_[64:128, :], in_=self.pw[0][0:64, :], func=AF.Copy),
             r=["pw0"], w=[rbn], tag="kswap2")
        p.do("ve", lambda e: e.tensor_mul(out=ra[:, :], in0=kr[:, :], in1=cst["rope_cos"][:, :]),
             r=[krn, "c_rope_cos"], w=[ran], tag="krope1")
        p.do("ve", lambda e: e.tensor_mul(out=rb_[:, :], in0=rb_[:, :], in1=cst["rope_sin"][:, :]),
             r=[rbn, "c_rope_sin"], w=[rbn], tag="krope2")
        p.do("ve", lambda e: e.tensor_add(out=sk[:, :], in0=ra[:, :], in1=rb_[:, :]),
             r=[ran, rbn], w=[skn], tag="krope3")
        ek, ekn = ekp[2]
        p.do("se", lambda e: e.activation(out=ek[:, :], in_=kr[:, :], func=AF.Exp),
             r=[krn], w=[ekn], tag="ksm1")
        for t in range(2):
            ps = self.pb[2][0:1, :] if t == 0 else self.pb[0][0:1, :]
            pr = "pb2" if t == 0 else "pb0"
            p.do("te", lambda e, t=t, ps=ps: e.matmul(
                    ps, cst["ones_col_bf"][:, :], ek[:, 512 * t:512 * (t + 1)],
                    start=True, stop=True),
                 r=[ekn, "c_ones_col_bf"], w=[pr], tag="ksm2")
            p.do("ve", lambda e, t=t, ps=ps: e.reciprocal(out=R(0)[:, 512 * t:512 * (t + 1)], in_=ps),
                 r=[pr], w=["rowS0"], tag="ksm3")
        p.do("ve", lambda e: e.tensor_copy(out=RB(0)[:, :], in_=R(0)[:, :]),
             r=["rowS0"], w=["rowB0"], tag="ksm4")
        ks, ksn = ekp[0]
        for t in range(2):
            bps = self.pw[1][:, :] if t == 0 else self.pb[2][:, :]
            bpr = "pw1" if t == 0 else "pb2"
            p.do("te", lambda e, t=t, bps=bps: e.matmul(
                    bps, cst["ones_row_bf"][:, :],
                    RB(0)[:, 512 * t:512 * (t + 1)], start=True, stop=True),
                 r=["rowB0", "c_ones_row_bf"], w=[bpr], tag="ksm5")
            p.do("ve", lambda e, t=t, ks=ks, bps=bps: e.tensor_mul(
                    out=ks[:, 512 * t:512 * (t + 1)], in0=ek[:, 512 * t:512 * (t + 1)], in1=bps),
                 r=[ekn, bpr], w=[ksn], tag="ksm6")
        p.do("se", lambda e: e.activation(out=rb_[:, :], in_=kr[:, :], func=AF.Abs),
             r=[krn], w=[rbn], tag="gl1")
        p.do("se", lambda e: e.activation(out=rb_[:, :], in_=rb_[:, :], func=AF.Exp, scale=-1.0),
             r=[rbn], w=[rbn], tag="gl2")
        p.do("se", lambda e: e.activation(out=glb[0][:, :], in_=rb_[:, :],
                                          func=AF.Ln, bias=cst["ones_col_f32"][:, :]),
             r=[rbn, "c_ones_col_f32"], w=[glb[1]], tag="gl3")
        p.do("ve", lambda e: e.tensor_scalar(out=rb_[:, :], in0=kr[:, :], scalar1=0.0,
                                             scalar2=None, op0=ALU.min),
             r=[krn], w=[rbn], tag="gl4")
        p.do("ve", lambda e: e.tensor_sub(out=glb[0][:, :], in0=rb_[:, :], in1=glb[0][:, :]),
             r=[rbn, glb[1]], w=[glb[1]], tag="gl5")
        for c in range(NT):
            p.do("ve", lambda e, c=c: e.tensor_tensor_scan(
                    out=glb[0][:, 128 * c:128 * (c + 1)], data0=glb[0][:, 128 * c:128 * (c + 1)],
                    data1=cst["zeros128"][:, :], initial=0.0, op0=ALU.add, op1=ALU.add),
                 r=[glb[1], "c_zeros128"], w=[glb[1]], tag="scan")
        eb, ebn = expb[s]
        p.do("se", lambda e, eb=eb: e.activation(out=eb[:, :], in_=glb[0][:, :], func=AF.Exp,
                                                 scale=1.0 / 16.0),
             r=[glb[1]], w=[ebn], tag="expb")
        enb, enbn = ekp[2]
        p.do("se", lambda e, enb=enb: e.activation(out=enb[:, :], in_=glb[0][:, :], func=AF.Exp,
                                                   scale=-1.0 / 16.0),
             r=[glb[1]], w=[enbn], tag="expnb")
        kt, ktn = ktil[s]
        p.do("ve", lambda e, kt=kt, ks=ks, enb=enb: e.tensor_mul(out=kt[:, :], in0=ks[:, :], in1=enb[:, :]),
             r=[ksn, enbn], w=[ktn], tag="ktil")
        for c in range(NT):
            p.do("se", lambda e, s=s, c=c: e.activation(
                    out=self.colA[:, 8 * s + c:8 * s + c + 1],
                    in_=glb[0][:, 128 * c + 127:128 * c + 128], func=AF.Exp, scale=1.0 / 16.0),
                 r=[glb[1]], w=["colA"], tag="bend")
        p.do("ve", lambda e, s=s: e.memset(self.colC[:, s:s + 1], 0.0),
             w=["colC"], tag="bcum0")
        for c in range(NT):
            p.do("se", lambda e, s=s, c=c: e.activation(
                    out=self.colB[:, 8 * s + c:8 * s + c + 1],
                    in_=self.colC[:, s:s + 1], func=AF.Exp, scale=1.0 / 16.0),
                 r=["colC"], w=["colB"], tag="bcum1")
            p.do("ve", lambda e, s=s, c=c: e.tensor_mul(
                    out=self.colB[:, 8 * s + c:8 * s + c + 1],
                    in0=self.colB[:, 8 * s + c:8 * s + c + 1], in1=cst["gate_col"][:, :]),
                 r=["colB", "c_gate_col"], w=["colB"], tag="bcum2")
            if c < NT - 1:
                p.do("ve", lambda e, s=s, c=c: e.tensor_add(
                        out=self.colC[:, s:s + 1], in0=self.colC[:, s:s + 1],
                        in1=glb[0][:, 128 * c + 127:128 * c + 128]),
                     r=["colC", glb[1]], w=["colC"], tag="bcum3")

    def chain_step(self, s, c, ktil, khat, ekp, vt_own):
        p, cst = self.p, self.cst
        kt, ktn = ktil[s]
        scr, scrn = ekp[1 + s % 2] if False else ekp[1]
        kh, khn = khat[s % 2]
        p.do("ve", lambda e, s=s, c=c, kt=kt, scr=scr: e.tensor_scalar(
                out=scr[:, 128 * (s % 2):128 * (s % 2) + 128], in0=kt[:, 128 * c:128 * (c + 1)],
                scalar1=self.colA[:, 8 * s + c:8 * s + c + 1], scalar2=None, op0=ALU.mult),
             r=[ktn, "colA"], w=[scrn], tag="khat1")
        p.do("te", lambda e, scr=scr, s=s: e.transpose(
                self.pbb[:, 128 * (s % 2):128 * (s % 2) + 128],
                scr[:, 128 * (s % 2):128 * (s % 2) + 128], cst["ident_bf16"][:, :]),
             r=[scrn, "c_ident_bf16"], w=["pbb"], tag="khat2")
        p.do("ve", lambda e, kh=kh, s=s: e.tensor_copy(
                out=kh[:, :], in_=self.pbb[:, 128 * (s % 2):128 * (s % 2) + 128]),
             r=["pbb"], w=[khn], tag="khat3")
        vt, vtn = vt_own[c]
        p.do("te", lambda e, kh=kh, vt=vt, s=s: e.matmul(
                self.pb[2][:, 0:128], kh[:, :], vt[:, 128 * s:128 * (s + 1)],
                start=True, stop=True),
             r=[khn, vtn], w=["pb2"], tag="chmm")
        p.do("ve", lambda e, s=s, c=c: e.tensor_scalar(
                out=self.S_f[s][:, :], in0=self.S_f[s][:, :],
                scalar1=self.colA[:, 8 * s + c:8 * s + c + 1], scalar2=None, op0=ALU.mult),
             r=["Sf%d" % s, "colA"], w=["Sf%d" % s], tag="chdec")
        p.do("ve", lambda e, s=s: e.tensor_add(out=self.S_f[s][:, :], in0=self.S_f[s][:, :],
                                               in1=self.pb[2][:, 0:128]),
             r=["Sf%d" % s, "pb2"], w=["Sf%d" % s], tag="chadd")
        if c < NT - 1:
            p.do("ve", lambda e, s=s, c=c: e.tensor_copy(out=self.S_b[s][c + 1][:, :],
                                                         in_=self.S_f[s][:, :]),
                 r=["Sf%d" % s], w=["Sb%d_%d" % (s, c + 1)], tag="chsnap")

    def front_a(self, h):
        a = self._aa
        p, cst = self.p, self.cst
        hb, R, RB = a["hb"], a["R"], a["RB"]
        s_ = h // 4
        qraw, qrn = hb[0]
        qsw, qswn = hb[1]
        sq, sqn = hb[2 + h % 2]
        ek, ekn = hb[1]
        qs, qsn = hb[0]
        p.dma("qd", qraw[:, :], self.d_qstage[128 * h:128 * (h + 1), :],
              r=["qst%d" % h], w=[qrn], tag="ldq")
        p.dma("qd", qsw[0:64, :], self.d_qstage[128 * h + 64:128 * (h + 1), :],
              r=["qst%d" % h], w=[qswn], tag="ldq2")
        p.dma("qd", qsw[64:128, :], self.d_qstage[128 * h:128 * h + 64, :],
              r=["qst%d" % h], w=[qswn], tag="ldq3")
        p.do("ve", lambda e, sq=sq: e.tensor_mul(out=sq[:, :], in0=qraw[:, :], in1=cst["rope_cos"][:, :]),
             r=[qrn, "c_rope_cos"], w=[sqn], tag="qrope1")
        p.do("ve", lambda e: e.tensor_mul(out=qsw[:, :], in0=qsw[:, :], in1=cst["rope_sin"][:, :]),
             r=[qswn, "c_rope_sin"], w=[qswn], tag="qrope2")
        p.do("ve", lambda e, sq=sq: e.tensor_add(out=sq[:, :], in0=sq[:, :], in1=qsw[:, :]),
             r=[sqn, qswn], w=[sqn], tag="qrope3")
        p.do("se", lambda e: e.activation(out=ek[:, :], in_=qraw[:, :], func=AF.Exp),
             r=[qrn], w=[ekn], tag="qsm1")
        for t in range(2):
            ps = self.pb[2][0:1, :] if t == 0 else self.pb[0][0:1, :]
            pr = "pb2" if t == 0 else "pb0"
            p.do("te", lambda e, t=t, ps=ps: e.matmul(
                    ps, cst["ones_col_bf"][:, :], ek[:, 512 * t:512 * (t + 1)],
                    start=True, stop=True),
                 r=[ekn, "c_ones_col_bf"], w=[pr], tag="qsm2")
            p.do("ve", lambda e, t=t, ps=ps: e.reciprocal(out=R(0)[:, 512 * t:512 * (t + 1)], in_=ps),
                 r=[pr], w=["rowS0"], tag="qsm3")
        p.do("ve", lambda e: e.tensor_copy(out=RB(0)[:, :], in_=R(0)[:, :]),
             r=["rowS0"], w=["rowB0"], tag="qsm4")
        for t in range(2):
            bps = self.pw[1][:, :] if t == 0 else self.pb[2][:, :]
            bpr = "pw1" if t == 0 else "pb2"
            p.do("te", lambda e, t=t, bps=bps: e.matmul(
                    bps, cst["ones_row_bf"][:, :],
                    RB(0)[:, 512 * t:512 * (t + 1)], start=True, stop=True),
                 r=["rowB0", "c_ones_row_bf"], w=[bpr], tag="qsm5")
            p.do("ve", lambda e, t=t, bps=bps, bpr=bpr: e.tensor_mul(
                    out=qs[:, 512 * t:512 * (t + 1)], in0=ek[:, 512 * t:512 * (t + 1)], in1=bps),
                 r=[ekn, bpr], w=[qsn], tag="qsm6")
        qt, qtn = hb[1]
        eb, ebn = a["expb"][s_]
        p.do("ve", lambda e, qt=qt, eb=eb: e.tensor_mul(out=qt[:, :], in0=qs[:, :], in1=eb[:, :]),
             r=[qsn, ebn], w=[qtn], tag="qtil")

    def gla_a(self, h):
        a = self._aa
        p, cst = self.p, self.cst
        s_ = h // 4
        oga = a["oga"][h % 2]
        hs = a["hs"]
        qt, qtn = a["hb"][1]
        kt, ktn = a["ktil"][s_]
        for c in range(NT):
            pt, ptn = hs[c % 2]
            p.do("te", lambda e, c=c, kt=kt, qt=qt: e.matmul(
                    self.pb[2][:, 0:128], kt[:, 128 * c:128 * (c + 1)], qt[:, 128 * c:128 * (c + 1)],
                    start=True, stop=True),
                 r=[ktn, qtn], w=["pb2"], tag="pt")
            p.do("ve", lambda e, pt=pt: e.tensor_mul(out=pt[:, :], in0=self.pb[2][:, 0:128],
                                                     in1=cst["tri_gla"][:, :]),
                 r=["pb2", "c_tri_gla"], w=[ptn], tag="ptmask")
            vt, vtn = a["vt_own"][c]
            p.do("te", lambda e, pt=pt, vt=vt, s_=s_: e.matmul(
                    self.pw[1][:, 0:128], vt[:, 128 * s_:128 * (s_ + 1)], pt[:, :],
                    start=True, stop=False),
                 r=[ptn, vtn], w=["pw1"], tag="oglam1")
            p.do("te", lambda e, s_=s_, c=c, qt=qt: e.matmul(
                    self.pw[1][:, 0:128], self.S_b[s_][c][:, :], qt[:, 128 * c:128 * (c + 1)],
                    start=False, stop=True),
                 r=["Sb%d_%d" % (s_, c), qtn], w=["pw1"], tag="oglam2")
            p.do("se", lambda e, c=c, oga=oga: e.activation(out=oga[0][:, 128 * c:128 * (c + 1)],
                                                   in_=self.pw[1][:, 0:128], func=AF.Copy, scale=0.5),
                 r=["pw1"], w=[oga[1]], tag="ogev")

    def quad_a(self, h, j):
        a = self._aa
        p, cst = self.p, self.cst
        s_ = h // 4
        sq, sqn = a["hb"][2 + h % 2]
        ep, R, RB = a["ep"], a["R"], a["RB"]
        sk_own, sk_par = a["sk_own"], a["sk_par"]
        vt_own, vt_par = a["vt_own"], a["vt_par"]
        sp = (2 * h + j) % 2
        sqsl = sq[:, 512 * j:512 * (j + 1)]

        def score(ck, j=j, sqsl=sqsl):
            if ck < 8:
                ksrc, ksrcn = sk_own[s_]
                koff = 128 * ck
            else:
                ksrc, ksrcn = sk_par[s_]
                koff = 128 * (ck - 8)
            pbi = ck % 2
            p.do("te", lambda e, ksrc=ksrc, koff=koff, sqsl=sqsl, pbi=pbi: e.matmul(
                    self.pb[pbi][:, :], ksrc[:, koff:koff + 128], sqsl,
                    start=True, stop=True),
                 r=[ksrcn, sqn], w=["pb%d" % pbi], tag="score")

        score(0)
        for ck in range(NKT):
            if ck + 1 < NKT:
                score(ck + 1)
            vsrc, vsrcn = vt_own[ck] if ck < 8 else vt_par[ck - 8]
            pbi = ck % 2
            band = (4 * j <= ck <= 4 * j + 3)
            epd, epdn = ep[ck % 3]
            bidx = 16 * j + ck
            p.do("se", lambda e, pbi=pbi, epd=epd, bidx=bidx: e.activation(
                    out=epd[:, :], in_=self.pb[pbi][:, :], func=AF.Exp,
                    scale=float(DK) ** -0.5, bias=cst["bias_tbl"][:, bidx:bidx + 1]),
                 r=["pb%d" % pbi, "c_bias_tbl"], w=[epdn], tag="expq")
            if band:
                i = ck - 4 * j
                p.do("ve", lambda e, epd=epd, i=i: e.tensor_mul(
                        out=epd[:, :], in0=epd[:, :],
                        in1=cst["band_masks"][:, 512 * i:512 * (i + 1)]),
                     r=[epdn, "c_band_masks"], w=[epdn], tag="bandm")
            p.do("te", lambda e, vsrc=vsrc, s_=s_, epd=epd, j=j, ck=ck: e.matmul(
                    self.pw[0][:, 512 * j:512 * (j + 1)], vsrc[:, 128 * s_:128 * (s_ + 1)],
                    epd[:, :], start=(ck == 0), stop=(ck == NKT - 1)),
                 r=[vsrcn, epdn], w=["pw0"], tag="ymm")
            p.do("te", lambda e, epd=epd, ck=ck: e.matmul(
                    self.pdn[0:1, :], cst["ones_col_bf"][:, :], epd[:, :],
                    start=(ck == 0), stop=(ck == NKT - 1)),
                 r=[epdn, "c_ones_col_bf"], w=["pdn"], tag="dnp")
        p.do("ve", lambda e, sp=sp: e.reciprocal(out=R(1 + sp)[:, 0:512], in_=self.pdn[0:1, :]),
             r=["pdn"], w=["rowS%d" % (1 + sp)], tag="dnrec")
        p.do("ve", lambda e, sp=sp: e.tensor_copy(out=RB(1)[:, 512 * sp:512 * (sp + 1)],
                                                  in_=R(1 + sp)[:, 0:512]),
             r=["rowS%d" % (1 + sp)], w=["rowB1"], tag="dnrecb")
        self.emit_pending()
        self.pending.append(self.fin_quad(h, j, sp, a["ysb"], a["oga"][h % 2],
                                          a["obig"][h], half=True))

    def fin_quad(self, h, j, sp, ysbp, oga, obig, half):
        p, cst = self.p, self.cst
        RB = lambda i: self.rowb[0:1, T * i:T * (i + 1)]
        ys, ysn = ysbp
        brow = cst["half_row_bf"] if half else cst["ones_row_bf"]
        brn = "c_half_row_bf" if half else "c_ones_row_bf"

        def emit():
            p.do("te", lambda e: e.matmul(self.pb[2][:, :], brow[:, :],
                                          RB(1)[:, 512 * sp:512 * (sp + 1)],
                                          start=True, stop=True),
                 r=["rowB1", brn], w=["pb2"], tag="dnbc")
            p.do("se", lambda e: e.activation(out=ys[:, :], in_=self.pw[0][:, 512 * j:512 * (j + 1)],
                                              func=AF.Copy),
                 r=["pw0"], w=[ysn], tag="yev")
            p.do("ve", lambda e: e.tensor_mul(out=ys[:, :], in0=ys[:, :], in1=self.pb[2][:, :]),
                 r=[ysn, "pb2"], w=[ysn], tag="ynorm")
            if oga is not None:
                p.do("ve", lambda e: e.tensor_add(out=obig[0][:, 512 * j:512 * (j + 1)],
                                                  in0=ys[:, :], in1=oga[0][:, 512 * j:512 * (j + 1)]),
                     r=[ysn, oga[1]], w=[obig[1]], tag="comb")
            else:
                bv = self.bcols_v
                p.do("ve", lambda e: e.tensor_scalar(
                        out=obig[0][:, 512 * j:512 * (j + 1)], in0=ys[:, :],
                        scalar1=bv[0][:, h:h + 1], scalar2=None, op0=ALU.add),
                     r=[ysn, bv[1]], w=[obig[1]], tag="bcomb")
        return emit

    def emit_pending(self):
        if self.pending:
            self.pending.pop(0)()

    def tap_big(self, name, tiles, f32=False):
        if name not in self.taps:
            return
        p = self.p
        out = self.nc.dram_tensor("tap_" + name, [len(tiles) * 128, T], F32,
                                  kind="ExternalOutput").ap()
        for i, (t, rn) in enumerate(tiles):
            dma = p.dma("gd", out[128 * i:128 * (i + 1), :], t[:, :],
                        r=[rn], w=["tap_%s_%d" % (name, i)], tag="tapd")
            self.final_dmas.append(dma)

    def tap_dram(self, name, dram, shape, deps):
        if name not in self.taps:
            return
        out = self.nc.dram_tensor("tap_" + name, shape, F32, kind="ExternalOutput").ap()
        dma = self.p.dma("gd", out, dram.ap(), r=deps, w=["tap_" + name], tag="tapdd")
        self.final_dmas.append(dma)

    def layernorm(self, idx):
        p = self.p
        cst = self.cst
        ar = self.ar
        xln = [ar.alloc("xln", [128, T], BF16) for _ in range(ND)]
        self.xln_end = ar.pos()
        xbs = [ar.alloc("lnxb", [128, T], BF16) for _ in range(2)]
        x2s = [ar.alloc("lnx2", [128, T], BF16) for _ in range(2)]
        for c in range(ND):
            xb, xbn = xbs[c % 2]
            x2, x2n = x2s[c % 2]
            p.do("ve", lambda e, c=c, xb=xb: e.tensor_copy(out=xb[:, :], in_=self.stream[c][:, :]),
                 r=["st%d" % c], w=[xbn], tag="lncast")
            p.do("se", lambda e, xb=xb, x2=x2: e.activation(out=x2[:, :], in_=xb[:, :], func=AF.Square),
                 r=[xbn], w=[x2n], tag="lnsq")
            for t in range(2):
                p.do("te", lambda e, c=c, t=t, xb=xb: e.matmul(
                        self.pb[t][0:1, :], cst["ones_col_bf"][:, :], xb[:, 512 * t:512 * (t + 1)],
                        start=(c == 0), stop=(c == ND - 1)),
                     r=[xbn, "c_ones_col_bf"], w=["pb%d" % t], tag="lnsum")
                sqps = self.pb[2][0:1, :] if t == 0 else self.pw[1][0:1, 0:512]
                sqr = "pb2" if t == 0 else "pw1"
                x2sl = x2[:, 512 * t:512 * (t + 1)]
                p.do("te", lambda e, c=c, x2sl=x2sl, sqps=sqps: e.matmul(
                        sqps, cst["ones_col_bf"][:, :], x2sl,
                        start=(c == 0), stop=(c == ND - 1)),
                     r=[x2n, "c_ones_col_bf"], w=[sqr], tag="lnsq2")
        R = lambda i: self.rowf[0:1, T * i:T * (i + 1)]
        RB = lambda i: self.rowb[0:1, T * i:T * (i + 1)]
        for t in range(2):
            p.do("ve", lambda e, t=t: e.tensor_scalar(
                    out=R(0)[:, 512 * t:512 * (t + 1)], in0=self.pb[t][0:1, :],
                    scalar1=1.0 / D, scalar2=None, op0=ALU.mult),
                 r=["pb%d" % t], w=["rowS0"], tag="lnrow")
            sqps2 = self.pb[2][0:1, :] if t == 0 else self.pw[1][0:1, 0:512]
            sqr2 = "pb2" if t == 0 else "pw1"
            p.do("ve", lambda e, t=t, sqps2=sqps2: e.tensor_scalar(
                    out=R(1)[:, 512 * t:512 * (t + 1)], in0=sqps2,
                    scalar1=1.0 / D, scalar2=None, op0=ALU.mult),
                 r=[sqr2], w=["rowS1"], tag="lnrow2")
        p.do("se", lambda e: e.activation(out=R(2)[:, :], in_=R(0)[:, :], func=AF.Square),
             r=["rowS0"], w=["rowS2"], tag="lnm2")
        p.do("ve", lambda e: e.scalar_tensor_tensor(out=R(1)[:, :], in0=R(1)[:, :],
                scalar=float(LN_EPS), in1=R(2)[:, :], op0=ALU.add, op1=ALU.subtract),
             r=["rowS1", "rowS2"], w=["rowS1"], tag="lnvar")
        p.do("se", lambda e: e.activation(out=R(2)[:, :], in_=R(1)[:, :], func=AF.Ln),
             r=["rowS1"], w=["rowS2"], tag="lnln")
        p.do("se", lambda e: e.activation(out=R(1)[:, :], in_=R(2)[:, :], func=AF.Exp, scale=-0.5),
             r=["rowS2"], w=["rowS1"], tag="lnexp")
        p.do("ve", lambda e: e.tensor_mul(out=R(2)[:, :], in0=R(0)[:, :], in1=R(1)[:, :]),
             r=["rowS0", "rowS1"], w=["rowS2"], tag="lnmr")
        p.do("ve", lambda e: e.tensor_copy(out=RB(0)[:, :], in_=R(1)[:, :]),
             r=["rowS1"], w=["rowB0"], tag="lnrb")
        p.do("ve", lambda e: e.tensor_copy(out=RB(1)[:, :], in_=R(2)[:, :]),
             r=["rowS2"], w=["rowB1"], tag="lnmb")
        for t in range(2):
            p.do("te", lambda e, t=t: e.matmul(
                    self.pw[0][:, 512 * t:512 * (t + 1)], cst["ones_row_bf"][:, :],
                    RB(0)[:, 512 * t:512 * (t + 1)], start=True, stop=True),
                 r=["rowB0", "c_ones_row_bf"], w=["pw0"], tag="lnbc1")
            mrps = self.pw[1][:, :] if t == 0 else self.pdn[:, :]
            mrr = "pw1" if t == 0 else "pdn"
            p.do("te", lambda e, t=t, mrps=mrps: e.matmul(
                    mrps, cst["ones_row_bf"][:, :],
                    RB(1)[:, 512 * t:512 * (t + 1)], start=True, stop=True),
                 r=["rowB1", "c_ones_row_bf"], w=[mrr], tag="lnbc2")
        tmps = [ar.alloc("lntmp", [128, T], F32) for _ in range(2)]
        for c in range(ND):
            tmp, tmpn = tmps[c % 2]
            xt, xtn = xln[c]
            p.do("ve", lambda e, c=c, tmp=tmp: e.tensor_mul(out=tmp[:, :], in0=self.stream[c][:, :], in1=self.pw[0][:, :]),
                 r=["st%d" % c, "pw0"], w=[tmpn], tag="lnn1")
            for t in range(2):
                mrps = self.pw[1][:, :] if t == 0 else self.pdn[:, :]
                mrr = "pw1" if t == 0 else "pdn"
                p.do("ve", lambda e, t=t, tmp=tmp, xt=xt, mrps=mrps: e.tensor_sub(
                        out=xt[:, 512 * t:512 * (t + 1)], in0=tmp[:, 512 * t:512 * (t + 1)], in1=mrps),
                     r=[tmpn, mrr], w=[xtn], tag="lnn2")
        return xln


def make_in_maps(inputs):
    x = inputs["L0_attn_a_hidden_states"]
    base = {
        "wq": inputs["L0_attn_a_Wq"], "wk": inputs["L0_attn_a_Wk"],
        "wv": inputs["L0_attn_a_Wv"], "wo": inputs["L0_attn_a_Wo"],
        "w1t": np.ascontiguousarray(
            inputs["L0_ffn_w1"].reshape(NH, 128, 16, 128).transpose(0, 3, 2, 1).reshape(HID, D)),
        "w2t": np.ascontiguousarray(
            inputs["L0_ffn_w2"].reshape(NH, 128, 16, 128).transpose(0, 3, 2, 1).reshape(HID, D)),
        "w3t": np.ascontiguousarray(inputs["L0_ffn_w3"].T),
    }
    for k in "qkvo":
        base["codes_" + k] = inputs["L0_attn_b_w%s_codes" % k]
        base["scales_" + k] = inputs["L0_attn_b_w%s_scales" % k]
        base["bias_" + k] = np.ascontiguousarray(inputs["L0_attn_b_b" + k].reshape(16, 128).T)
    w1f = base["w1t"].reshape(-1)
    w2f = base["w2t"].reshape(-1)
    w3f = base["w3t"].reshape(-1)
    n8 = w1f.size // 8
    maps = []
    for core in range(8):
        b, half = core // 2, core % 2
        m = dict(base)
        m["x_fm"] = np.ascontiguousarray(x[b, 1024 * half:1024 * (half + 1), :].T)
        m["wsl0"] = np.ascontiguousarray(w1f[n8 * core:n8 * (core + 1)].reshape(128, 11264))
        m["wsl1"] = np.ascontiguousarray(w2f[n8 * core:n8 * (core + 1)].reshape(128, 11264))
        m["wsl2"] = np.ascontiguousarray(w3f[n8 * core:n8 * (core + 1)].reshape(128, 11264))
        m.update(host_constants(core))
        maps.append(m)
    return maps


def kernel(**inputs):
    bld = Build(taps=_TAPS)
    in_maps = make_in_maps(inputs)
    res = run_bass_kernel_spmd(bld.nc, in_maps, core_ids=list(range(8)))
    out = np.empty((4, 2048, 2048), dtype=np.float32)
    for core in range(8):
        b, half = core // 2, core % 2
        out[b, 1024 * half:1024 * (half + 1), :] = res.results[core]["y"].T
    if _TAPS:
        taps = {}
        for name in _TAPS:
            taps[name] = [res.results[c].get("tap_" + name) for c in range(8)]
        return out, taps
    return out
